# version 01
# baseline (speedup 1.0000x reference)
import math
import numpy as np
import ml_dtypes
import concourse.bass as bass
import concourse.mybir as mybir
from concourse.bass_utils import run_bass_kernel_spmd

F32 = mybir.dt.float32
BF16 = mybir.dt.bfloat16
AF = mybir.ActivationFunctionType
ALU = mybir.AluOpType
AX = mybir.AxisListType

BF = ml_dtypes.bfloat16

D = 2048
T = 1024
NTILE = D // 128
S = 2048
LAMBDA_INIT = 0.8 - 0.6 * math.exp(-0.3 * 2)

ENGINES = ("pe", "act", "dve", "pool", "sync")


class Builder:

    def __init__(self):
        self.q = {e: [] for e in ENGINES}
        self.cnt = {}
        self.order = []

    def sem(self, name):
        if name not in self.cnt:
            self.cnt[name] = 0
            self.order.append(name)
        return name

    def v(self, name):
        self.sem(name)
        return self.cnt[name]

    def op(self, eng, fn, waits=(), inc=None, n=1):
        w = tuple((s, v) for (s, v) in waits if v > 0)
        for s, _ in w:
            self.sem(s)
        if inc is not None:
            self.sem(inc)
            self.cnt[inc] += n
            post = (inc, n)
        else:
            post = None
        self.q[eng].append((fn, w, post))
        return self.cnt[inc] if inc is not None else None

    def emit(self, nc, sems):
        with nc.Block() as block:
            def mk(name):
                def run(e):
                    for fn, waits, post in self.q[name]:
                        for s, v in waits:
                            e.wait_ge(sems[s], v)
                        ins = fn(e)
                        if post is not None:
                            assert ins is not None, "inc-op returned None"
                            ins.then_inc(sems[post[0]], post[1])
                return run
            block.tensor(mk("pe"))
            block.scalar(mk("act"))
            block.vector(mk("dve"))
            block.gpsimd(mk("pool"))
            block.sync(mk("sync"))


def own_pos(h):
    if h == 0:
        return np.concatenate([np.arange(1536, 2048), np.arange(0, 512)])
    return np.concatenate([np.arange(1024, 1536), np.arange(512, 1024)])


def rope0_tables(h):
    pos = own_pos(h).astype(np.float64)
    invf = 10000.0 ** (-np.arange(64) / 64.0)
    fr = pos[None, :] * invf[:, None]
    c = np.cos(fr); s = np.sin(fr)
    C = np.concatenate([c, c], 0).astype(BF)
    blk = np.concatenate([s[32:64], -s[0:32]], 0)
    S2 = np.concatenate([blk, blk], 0).astype(BF)
    return C, S2


def rope1_tables(h):
    pos = own_pos(h).astype(np.float64)
    invf = 1.0 / 10000.0 ** (np.arange(0, 64, 2) / 64.0)
    fr = pos[None, :] * invf[:, None]
    c = np.cos(fr); s = np.sin(fr)
    C1 = np.concatenate([c, c], 0).astype(BF)
    SS = np.concatenate([-s, s], 0).astype(BF)
    return C1, SS


def mask_tables(h):
    gs = {0: {0: 1536, 1: 0}, 1: {0: 1024, 1: 512}}[h]
    out = np.zeros((16, 128, 512), dtype=BF)
    i = 0
    for slot, kts in ((0, range(8, 16)), (1, range(0, 8))):
        g = gs[slot]
        for kt in kts:
            k = kt * 128 + np.arange(128)[:, None]
            q = g + np.arange(512)[None, :]
            out[i] = (k <= q).astype(np.float32).astype(BF)
            i += 1
    return out


def qpos(j):
    return 2 * (j // 4) + (j % 2), 64 * ((j // 2) % 2)


def q_perm_rows():
    order = [None] * 32
    for j in range(32):
        t, base = qpos(j)
        order[2 * t + base // 64] = j
    return order


MASKED = {}
for _i, (_slot, _kt) in enumerate([(0, k) for k in range(8, 16)] +
                                  [(1, k) for k in range(0, 8)]):
    MASKED[(_slot, _kt)] = _i
SLOT_KTS = {0: list(range(16)), 1: list(range(8))}


def chunked_cols(dram_ap, col0, ncols):
    return dram_ap[:, col0:col0 + ncols].rearrange("(c p) n -> p c n", p=128)


def sb3(t, nch, ncols):
    return t.ap().rearrange("p (c n) -> p c n", c=nch)


class Prog:
    def __init__(self, name):
        self.nc = bass.Bass("TRN2", target_bir_lowering=False, dynamic_dma_scratch_size=16384)
        self.b = Builder()
        self.name = name
        self.banks = [self.nc.alloc_psum_tensor("bank%d" % i, [128, 512], F32)
                      for i in range(8)]
        self.out_waits = []

    def sb(self, name, shape, dtype):
        return self.nc.alloc_sbuf_tensor(name, shape, dtype)

    def din(self, name, shape, dtype):
        return self.nc.dram_tensor(name, shape, dtype, kind="ExternalInput").ap()

    def dout(self, name, shape, dtype):
        return self.nc.dram_tensor(name, shape, dtype, kind="ExternalOutput").ap()

    def make_common(self, rope=True):
        nc = self.nc
        self.c_ones_col = self.sb("ones_col", [128, 1], BF16)
        self.c_ones_row = self.sb("ones_row", [1, 128], BF16)
        if rope:
            self.qtmp = self.sb("qtmp", [128, 2 * T], BF16)
            self.ropetmp = self.sb("ropetmp", [128, T], BF16)
        self.ln_small = self.sb("ln_small", [1, 2048], BF16)
        b = self.b
        b.op("dve", lambda e: e.memset(self.c_ones_col.ap()[:, :], 1.0))
        b.op("dve", lambda e: e.memset(self.c_ones_row.ap()[:, :], 1.0),
             inc="dve")

    def finish(self):
        b, nc = self.b, self.nc
        import contextlib
        stack = contextlib.ExitStack()
        sems = {}
        for name in b.order:
            sems[name] = stack.enter_context(nc.semaphore(name))
        b.emit(nc, sems)
        stack.close()
        return nc



def emit_layernorm(p, xt3, xn3, st, rbuf, entry=(), chunk_scale=None):
    b = p.b
    A = p.banks
    for c in range(NTILE):
        b.op("act", lambda e, c=c: e.copy(out=xn3[:, c, :], in_=xt3[:, c, :]),
             waits=(entry if c == 0 else ()),
             inc="act" if c == NTILE - 1 else None)
    a1 = b.v("act")
    for half in range(2):
        for c in range(NTILE):
            b.op("pe", lambda e, c=c, half=half: e.matmul(
                A[half].ap()[0:1, :], p.c_ones_col.ap()[:, 0:1],
                xn3[:, c, half * 512:(half + 1) * 512],
                start=(c == 0), stop=(c == NTILE - 1)),
                waits=((("act", a1),) if (half == 0 and c == 0) else ()),
                inc="pe" if (half == 1 and c == NTILE - 1) else None)
    p1 = b.v("pe")
    for c in range(NTILE):
        b.op("act", lambda e, c=c: e.activation(
            out=xn3[:, c, :], in_=xt3[:, c, :], func=AF.Square),
            waits=((("pe", p1),) if c == 0 else ()),
            inc="act" if c == NTILE - 1 else None)
    a2 = b.v("act")
    if getattr(p, "dbg_sq", None) is not None:
        _dsq, _dxt = p.dbg_sq, p.dbg_xt
        b.op("sync", lambda e, _dsq=_dsq: e.dma_start(
            out=_dsq.rearrange("(c p) t -> p c t", p=128), in_=xn3),
            waits=(("act", a2),), inc="dbgq", n=16)
        b.op("sync", lambda e, _dxt=_dxt: e.dma_start(
            out=_dxt.rearrange("(c p) t -> p c t", p=128), in_=xt3),
            inc="dbgq", n=16)
        dv = b.v("dbgq")
        b.op("pe", lambda e: None, waits=(("dbgq", dv),))
        p.dbg_sq = None
    for half in range(2):
        for c in range(NTILE):
            b.op("pe", lambda e, c=c, half=half: e.matmul(
                A[6 + half].ap()[0:1, :], p.c_ones_col.ap()[:, 0:1],
                xn3[:, c, half * 512:(half + 1) * 512],
                start=(c == 0), stop=(c == NTILE - 1)),
                waits=((("act", a2),) if (half == 0 and c == 0) else ()),
                inc="pe" if (half == 1 and c == NTILE - 1) else None)
    p2 = b.v("pe")
    sm = st[0:1, 0:1024]; ex = st[0:1, 1024:2048]; var = st[0:1, 2048:3072]
    for half in range(2):
        b.op("dve", lambda e, half=half: e.tensor_scalar_mul(
            sm[:, half * 512:(half + 1) * 512], A[half].ap()[0:1, :], 1.0 / 2048.0),
            waits=((("pe", p2),) if half == 0 else ()))
        b.op("dve", lambda e, half=half: e.tensor_scalar_mul(
            ex[:, half * 512:(half + 1) * 512], A[6 + half].ap()[0:1, :],
            1.0 / 2048.0), inc="dve" if half == 1 else None)
    d1 = b.v("dve")
    b.op("dve", lambda e: e.tensor_mul(var, sm, sm), waits=(("dve", d1),),
         inc="dve")
    d1 = b.v("dve")
    b.op("dve", lambda e: e.tensor_scalar(var, var, -1.0, 1e-5, ALU.mult,
                                          ALU.add), waits=(("dve", d1),),
         inc="dve")
    d1 = b.v("dve")
    b.op("dve", lambda e: e.tensor_add(var, var, ex), waits=(("dve", d1),),
         inc="dve")
    d1 = b.v("dve")
    b.op("act", lambda e: e.activation(out=var, in_=var, func=AF.Sqrt),
         waits=(("dve", d1),), inc="act")
    a3 = b.v("act")
    b.op("dve", lambda e: e.reciprocal(var, var), waits=(("act", a3),),
         inc="dve")
    d1 = b.v("dve")
    b.op("dve", lambda e: e.tensor_mul(sm, sm, var), waits=(("dve", d1),),
         inc="dve")
    d1 = b.v("dve")
    b.op("dve", lambda e: e.tensor_copy(p.ln_small.ap()[0:1, 0:1024], var),
         waits=(("dve", d1),))
    b.op("dve", lambda e: e.tensor_copy(p.ln_small.ap()[0:1, 1024:2048], sm),
         inc="dve")
    d2 = b.v("dve")
    for j in range(4):
        b.op("pe", lambda e, j=j: e.matmul(
            A[4 + j].ap()[:, :], p.c_ones_row.ap()[0:1, :],
            p.ln_small.ap()[0:1, j * 512:(j + 1) * 512], start=True, stop=True),
            waits=((("dve", d2),) if j == 0 else ()),
            inc="pe" if j == 3 else None)
    p3 = b.v("pe")
    for j in range(4):
        b.op("act", lambda e, j=j: e.copy(
            out=rbuf[:, j * 512:(j + 1) * 512], in_=A[4 + j].ap()[:, :]),
            waits=((("pe", p3),) if j == 0 else ()),
            inc="act" if j == 3 else None)
    a4 = b.v("act")
    for c in range(NTILE):
        b.op("dve", lambda e, c=c: e.tensor_mul(
            xt3[:, c, :], xt3[:, c, :], rbuf[:, 0:1024]),
            waits=((("act", a4),) if c == 0 else ()))
        b.op("dve", lambda e, c=c: e.tensor_sub(
            xn3[:, c, :], xt3[:, c, :], rbuf[:, 1024:2048]),
            inc="dve" if (chunk_scale is None and c == NTILE - 1) else None)
    if chunk_scale is not None:
        for c in range(NTILE):
            b.op("dve", lambda e, c=c: e.tensor_scalar(
                xn3[:, c, :], xn3[:, c, :], chunk_scale[:, c:c + 1], None,
                ALU.mult), inc="dve" if c == NTILE - 1 else None)
    return b.v("dve")


def emit_rope(p, kind, src, dst):
    b = p.b
    tmp = p.ropetmp.ap()
    if kind == "l0":
        C = p.c_rope0c.ap(); S2 = p.c_rope0s.ap()
        b.op("dve", lambda e: e.tensor_mul(dst, src, C))
        for g in (0, 64):
            b.op("dve", lambda e, g=g: e.tensor_mul(
                tmp[g:g + 32, :], src[g + 32:g + 64, :], S2[g + 32:g + 64, :]))
            b.op("dve", lambda e, g=g: e.tensor_mul(
                tmp[g + 32:g + 64, :], src[g:g + 32, :], S2[g:g + 32, :]))
        b.op("dve", lambda e: e.tensor_add(dst, dst, tmp[:, :]), inc="dve")
    else:
        C = p.c_rope1c.ap(); SS = p.c_rope1s.ap()
        b.op("dve", lambda e: e.tensor_copy(dst[64:128, :], src[64:128, :]))
        b.op("dve", lambda e: e.tensor_mul(dst[0:32, :], src[0:32, :], C[0:32, :]))
        b.op("dve", lambda e: e.tensor_mul(
            tmp[0:32, :], src[32:64, :], SS[32:64, :]))
        b.op("dve", lambda e: e.tensor_add(dst[0:32, :], dst[0:32, :], tmp[0:32, :]))
        b.op("dve", lambda e: e.tensor_mul(
            dst[32:64, :], src[32:64, :], C[32:64, :]))
        b.op("dve", lambda e: e.tensor_mul(
            tmp[32:64, :], src[0:32, :], SS[0:32, :]))
        b.op("dve", lambda e: e.tensor_add(
            dst[32:64, :], dst[32:64, :], tmp[32:64, :]), inc="dve")
    return b.v("dve")


def emit_proj_fm(p, w_dram, xn, out_slab, dout, wbuf, xn_ready, rope=None,
                 sem_prefix="q", post_tile=None):
    b = p.b
    ndt = dout // 128
    xn3 = sb3(xn, NTILE, T)
    out3 = sb3(out_slab, max(ndt, 1), T)
    wb3 = wbuf.ap()[:, 0:2 * NTILE * 128].rearrange("p (c n) -> p c n",
                                                    c=2 * NTILE)
    qt3 = sb3(p.qtmp, 2, T)
    A = p.banks
    evac = {}
    rope_m = {}
    wsem = sem_prefix + "_wdma"
    entry_act = b.v("act")
    entry_pe = b.v("pe")
    for dt in range(ndt):
        buf = dt % 2
        waits = []
        if dt < 2:
            waits.append(("pe", entry_pe))
        if dt >= 2:
            waits.append(("pe", evac[dt - 2][0]))
        b.op("pool", lambda e, dt=dt, buf=buf: e.dma_start(
            out=wb3[:, buf * NTILE:(buf + 1) * NTILE, :],
            in_=chunked_cols(w_dram, dt * 128, 128)),
            waits=waits, inc=wsem, n=16)
        wv = b.v(wsem)
        pe_w = [(wsem, wv)]
        if dt == 0 and xn_ready:
            pe_w.append(xn_ready)
        if dt < 2:
            pe_w.append(("act", entry_act))
        if dt >= 2:
            pe_w.append(("act", evac[dt - 2][1]))
        for tch in range(2):
            bank = A[2 * buf + tch]
            for kc in range(NTILE):
                b.op("pe", lambda e, buf=buf, tch=tch, kc=kc, bank=bank: e.matmul(
                    bank.ap()[:, :], wb3[:, buf * NTILE + kc, :],
                    xn3[:, kc, tch * 512:(tch + 1) * 512],
                    start=(kc == 0), stop=(kc == NTILE - 1)),
                    waits=pe_w if (tch == 0 and kc == 0) else (),
                    inc="pe" if (tch == 1 and kc == NTILE - 1) else None)
        pm = b.v("pe")
        for tch in range(2):
            bank = A[2 * buf + tch]
            if rope:
                o = qt3[:, buf, tch * 512:(tch + 1) * 512]
            else:
                o = out3[:, dt, tch * 512:(tch + 1) * 512]
            w = [("pe", pm)] if tch == 0 else []
            if rope and dt >= 2 and tch == 0:
                w.append(("dve", rope_m[dt - 2]))
            b.op("act", lambda e, bank=bank, o=o: e.copy(out=o, in_=bank.ap()[:, :]),
                 waits=w, inc="act" if tch == 1 else None)
        am = b.v("act")
        evac[dt] = (pm, am)
        if rope:
            b.op("dve", lambda e: None, waits=(("act", am),))
            rope_m[dt] = emit_rope(p, rope, qt3[:, buf, :], out3[:, dt, :])
        if post_tile is not None:
            post_tile(dt, am)
    return ("dve", b.v("dve")) if rope else ("act", b.v("act"))


def emit_vproj_tm(p, w_dram, xn, out_slab, dout, wbuf, xn_ready,
                  sem_prefix="v"):
    b = p.b
    xn3 = sb3(xn, NTILE, T)
    nvch = dout // 512
    out3 = sb3(out_slab, 8, dout)
    wb3 = sb3(wbuf, NTILE, 512)
    A = p.banks
    wsem = sem_prefix + "_wdma"
    evac = {}
    entry_act = b.v("act")
    entry_pe = b.v("pe")
    for nch in range(nvch):
        dma_w = []
        if nch == 0:
            dma_w.append(("pe", entry_pe))
            if xn_ready:
                dma_w.append(xn_ready)
        if nch > 0:
            dma_w.append(("pe", evac_pe_last))
        b.op("pool", lambda e, nch=nch: e.dma_start(
            out=wb3[:, 0:NTILE, :],
            in_=chunked_cols(w_dram, nch * 512, 512)),
            waits=dma_w, inc=wsem, n=16)
        wv = b.v(wsem)
        for tt in range(8):
            bank = A[4 + tt % 2]
            pe_w = []
            if tt == 0:
                pe_w.append((wsem, wv))
                if nch == 0:
                    pe_w.append(("act", entry_act))
                    if xn_ready:
                        pe_w.append(xn_ready)
            key = (nch * 8 + tt) - 2
            if key >= 0:
                pe_w.append(("act", evac[key]))
            for kc in range(NTILE):
                b.op("pe", lambda e, tt=tt, kc=kc, bank=bank: e.matmul(
                    bank.ap()[:, :], xn3[:, kc, tt * 128:(tt + 1) * 128],
                    wb3[:, kc, :],
                    start=(kc == 0), stop=(kc == NTILE - 1)),
                    waits=pe_w if kc == 0 else (),
                    inc="pe" if kc == NTILE - 1 else None)
            pm = b.v("pe")
            b.op("act", lambda e, tt=tt, nch=nch, bank=bank: e.copy(
                out=out3[:, tt, nch * 512:(nch + 1) * 512], in_=bank.ap()[:, :]),
                waits=(("pe", pm),), inc="act")
            evac[nch * 8 + tt] = b.v("act")
        evac_pe_last = b.v("pe")
    return ("act", b.v("act"))



def build_l0_qkv(debug=False):
    p = Prog("l0qkv")
    b = p.b
    x_in = p.din("x", [D, T], F32)
    wqT = p.din("wqT", [D, D], F32)
    wkT = p.din("wkT", [D, 1024], F32)
    wvT = p.din("wvT", [D, 1024], F32)
    rc = p.din("rope0c", [128, T], BF16)
    rs = p.din("rope0s", [128, T], BF16)
    q_out = p.dout("q", [D, T], BF16)
    kv_out = p.dout("kv", [2048, T], BF16)
    xn_out = p.dout("xn_dbg", [D, T], BF16) if debug else None

    xt = p.sb("xt", [128, NTILE * T], F32)
    xn = p.sb("xn", [128, NTILE * T], BF16)
    qs = p.sb("qs", [128, NTILE * T], BF16)
    ks = p.sb("ks", [128, 8 * T], BF16)
    vs = p.sb("vs", [128, 8 * T], BF16)
    wbuf = p.sb("wbuf", [128, 2 * NTILE * 128], BF16)
    wvbuf = p.sb("wvbuf", [128, NTILE * 512], BF16)
    stats = wvbuf.ap().bitcast(F32)[0:1, 0:3072]
    rbuf = p.sb("rbuf", [128, 2 * T], BF16)
    p.make_common()
    p.c_rope0c = p.sb("rope0c_t", [128, T], BF16)
    p.c_rope0s = p.sb("rope0s_t", [128, T], BF16)

    b.op("sync", lambda e: e.dma_start(
        out=sb3(xt, NTILE, T), in_=x_in.rearrange("(c p) t -> p c t", p=128)),
        inc="adma", n=16)
    b.op("sync", lambda e: e.dma_start(out=p.c_rope0c.ap()[:, :], in_=rc),
         inc="adma", n=16)
    b.op("sync", lambda e: e.dma_start(out=p.c_rope0s.ap()[:, :], in_=rs),
         inc="adma", n=16)
    ad = b.v("adma")
    xn_ready = emit_layernorm(p, sb3(xt, NTILE, T), sb3(xn, NTILE, T),
                              stats, rbuf.ap(), entry=(("adma", ad),))

    qm = emit_proj_fm(p, wqT, xn, qs, D, wbuf, ("dve", xn_ready), rope="l0",
                      sem_prefix="q")
    km = emit_proj_fm(p, wkT, xn, ks, 1024, wbuf, None, rope="l0",
                      sem_prefix="k")
    vm = emit_vproj_tm(p, wvT, xn, vs, 1024, wvbuf, None, sem_prefix="v")

    b.op("sync", lambda e: e.dma_start(
        out=q_out.rearrange("(c p) t -> p c t", p=128), in_=sb3(qs, NTILE, T)),
        waits=(qm,), inc="osem", n=16)
    b.op("sync", lambda e: e.dma_start(
        out=kv_out[0:1024, :].rearrange("(c p) t -> p c t", p=128),
        in_=sb3(ks, 8, T)), waits=(km,), inc="osem", n=16)
    b.op("sync", lambda e: e.dma_start(
        out=kv_out[1024:2048, :].rearrange("(c p) t -> p c t", p=128),
        in_=sb3(vs, 8, T)), waits=(vm,), inc="osem", n=16)
    nout = 3
    if debug:
        b.op("sync", lambda e: e.dma_start(
            out=xn_out.rearrange("(c p) t -> p c t", p=128), in_=sb3(xn, NTILE, T)),
            waits=(("dve", xn_ready),), inc="osem", n=16)
        nout = 4
    b.op("sync", lambda e: None, waits=(("osem", nout * 16),))
    return p



def shard_x(x_full):
    out = []
    for c in range(8):
        bb, h = divmod(c, 2)
        out.append(np.ascontiguousarray(x_full[bb][own_pos(h)].T.astype(np.float32)))
    return out


def kernel(**inputs):
    raise NotImplementedError("assembled incrementally; see scratch drivers")



SQ8192 = 1.0 / math.sqrt(8192.0)


def emit_l0_bounds(p, qs, ks, scr, smax, negBs, ready):
    b = p.b
    A = p.banks
    q3 = qs.rearrange("p (c t) -> p c t", c=NTILE)
    k3 = ks.rearrange("p (c t) -> p c t", c=8)
    sm = smax.ap()
    prev_pe = 0
    for hh in range(NTILE):
        w = list(ready) if hh == 0 else [("pe", prev_pe)]
        b.op("act", lambda e, hh=hh: e.activation(
            out=scr[:, 0:1024], in_=q3[:, hh, :], func=AF.Square),
            waits=w, inc="act")
        am = b.v("act")
        dwait = b.v("dve")
        for g in range(2):
            for tch in range(2):
                b.op("pe", lambda e, g=g, tch=tch: e.matmul(
                    A[4 + 2 * g + tch].ap()[0:1, :],
                    p.c_ones_col.ap()[g * 64:(g + 1) * 64, 0:1],
                    scr[g * 64:(g + 1) * 64, tch * 512:(tch + 1) * 512],
                    start=True, stop=True),
                    waits=((("act", am), ("dve", dwait))
                           if (g == 0 and tch == 0) else ()),
                    inc="pe" if (g == 1 and tch == 1) else None)
        prev_pe = b.v("pe")
        for g in range(2):
            j = q_perm_rows()[2 * hh + g]
            b.op("dve", lambda e, g=g, j=j: e.reduce_max(
                sm[0:1, j:j + 1], A[4 + 2 * g].ap()[0:1, :], axis=AX.X),
                waits=((("pe", prev_pe),) if g == 0 else ()))
            b.op("dve", lambda e, g=g, j=j: e.reduce_max(
                sm[0:1, 64 + j:65 + j], A[5 + 2 * g].ap()[0:1, :], axis=AX.X),
                inc="dve" if g == 1 else None)
    for kk in range(8):
        b.op("act", lambda e, kk=kk: e.activation(
            out=scr[:, :], in_=k3[:, kk, :], func=AF.Square),
            waits=(("pe", prev_pe),), inc="act")
        am = b.v("act")
        for g in range(2):
            dwait = b.v("dve")
            for tch in range(4):
                b.op("pe", lambda e, g=g, tch=tch: e.matmul(
                    A[4 + tch].ap()[0:1, :],
                    p.c_ones_col.ap()[g * 64:(g + 1) * 64, 0:1],
                    scr[g * 64:(g + 1) * 64, tch * 512:(tch + 1) * 512],
                    start=True, stop=True),
                    waits=((("act", am), ("dve", dwait)) if tch == 0 else ()),
                    inc="pe" if tch == 3 else None)
            prev_pe = b.v("pe")
            j = 2 * kk + g
            for tch in range(4):
                slot = (32 + j) if tch == 0 else (96 + j + 32 * (tch - 1))
                b.op("dve", lambda e, tch=tch, slot=slot: e.reduce_max(
                    sm[0:1, slot:slot + 1], A[4 + tch].ap()[0:1, :], axis=AX.X),
                    waits=((("pe", prev_pe),) if tch == 0 else ()),
                    inc="dve" if tch == 3 else None)
    dm = b.v("dve")
    b.op("dve", lambda e: e.tensor_max(sm[0:1, 0:48], sm[0:1, 0:48],
                                       sm[0:1, 64:112]),
         waits=(("dve", dm),), inc="dve")
    dm = b.v("dve")
    b.op("dve", lambda e: e.tensor_max(sm[0:1, 32:48], sm[0:1, 32:48],
                                       sm[0:1, 128:144]),
         waits=(("dve", dm),), inc="dve")
    dm = b.v("dve")
    b.op("dve", lambda e: e.tensor_max(sm[0:1, 32:48], sm[0:1, 32:48],
                                       sm[0:1, 160:176]),
         waits=(("dve", dm),), inc="dve")
    dm = b.v("dve")
    b.op("act", lambda e: e.activation(out=sm[0:1, 64:112], in_=sm[0:1, 0:48],
                                       func=AF.Sqrt),
         waits=(("dve", dm),), inc="act")
    am = b.v("act")
    b.op("dve", lambda e: e.tensor_copy(
        sm[0:1, 128:160].rearrange("p (a b) -> p a b", b=2)[:, :, 0:1],
        sm[0:1, 96:112].rearrange("p (a b) -> p a b", b=1)),
        waits=(("act", am),))
    b.op("dve", lambda e: e.tensor_copy(
        sm[0:1, 128:160].rearrange("p (a b) -> p a b", b=2)[:, :, 1:2],
        sm[0:1, 96:112].rearrange("p (a b) -> p a b", b=1)), inc="dve")
    dm = b.v("dve")
    b.op("dve", lambda e: e.scalar_tensor_tensor(
        sm[0:1, 160:192], sm[0:1, 64:96], -0.125, sm[0:1, 128:160],
        ALU.mult, ALU.mult), waits=(("dve", dm),), inc="dve")
    dm = b.v("dve")
    b.op("dve", lambda e: e.tensor_scalar_add(sm[0:1, 160:192],
                                              sm[0:1, 160:192], 44.0),
         waits=(("dve", dm),), inc="dve")
    dm = b.v("dve")
    b.op("dve", lambda e: e.tensor_copy(p.ln_small.ap()[0:1, 0:32],
                                        sm[0:1, 160:192]),
         waits=(("dve", dm),), inc="dve")
    dm = b.v("dve")
    b.op("pe", lambda e: e.matmul(A[4].ap()[:, 0:32], p.c_ones_row.ap()[0:1, :],
                                  p.ln_small.ap()[0:1, 0:32],
                                  start=True, stop=True),
         waits=(("dve", dm),), inc="pe")
    pm = b.v("pe")
    b.op("act", lambda e: e.copy(out=negBs.ap()[:, 0:32], in_=A[4].ap()[:, 0:32]),
         waits=(("pe", pm),), inc="act")
    return b.v("act")


def emit_l0_lambda(p, lam, sublnw, lsc, lnb, sw):
    b = p.b
    A = p.banks
    sc = lsc
    b.op("sync", lambda e: e.dma_start(out=sc[0:1, 0:256], in_=lam),
         inc="adma", n=16)
    b.op("sync", lambda e: e.dma_start(out=sw[:, 0:1], in_=sublnw),
         inc="adma", n=16)
    ad = b.v("adma")
    b.op("dve", lambda e: e.tensor_mul(sc[0:1, 0:64], sc[0:1, 0:64],
                                       sc[0:1, 64:128]),
         waits=(("adma", ad),))
    b.op("dve", lambda e: e.tensor_mul(sc[0:1, 128:192], sc[0:1, 128:192],
                                       sc[0:1, 192:256]), inc="dve")
    d = b.v("dve")
    b.op("dve", lambda e: e.reduce_sum(sc[0:1, 64:65], sc[0:1, 0:64], axis=AX.X),
         waits=(("dve", d),))
    b.op("dve", lambda e: e.reduce_sum(sc[0:1, 65:66], sc[0:1, 128:192],
                                       axis=AX.X), inc="dve")
    d = b.v("dve")
    b.op("act", lambda e: e.activation(out=sc[0:1, 66:68], in_=sc[0:1, 64:66],
                                       func=AF.Exp),
         waits=(("dve", d),), inc="act")
    a = b.v("act")
    b.op("dve", lambda e: e.tensor_sub(sc[0:1, 68:69], sc[0:1, 67:68],
                                       sc[0:1, 66:67]),
         waits=(("act", a),), inc="dve")
    d = b.v("dve")
    b.op("dve", lambda e: e.tensor_scalar_add(sc[0:1, 68:69], sc[0:1, 68:69],
                                              -LAMBDA_INIT),
         waits=(("dve", d),), inc="dve")
    d = b.v("dve")
    b.op("dve", lambda e: e.tensor_copy(p.ln_small.ap()[0:1, 40:41],
                                        sc[0:1, 68:69]),
         waits=(("dve", d),))
    b.op("dve", lambda e: e.tensor_scalar_mul(sw[:, 0:1], sw[:, 0:1],
                                              1.0 - LAMBDA_INIT), inc="dve")
    d = b.v("dve")
    b.op("pe", lambda e: e.matmul(A[4].ap()[:, 32:33], p.c_ones_row.ap()[0:1, :],
                                  p.ln_small.ap()[0:1, 40:41],
                                  start=True, stop=True),
         waits=(("dve", d),), inc="pe")
    pm = b.v("pe")
    b.op("act", lambda e: e.copy(out=lnb[:, 0:1], in_=A[4].ap()[:, 32:33]),
         waits=(("pe", pm),), inc="act")
    return b.v("act")


def emit_l0_attention(p, qs, ks, vs, attn_c, masks, ebuf, negBs, lnb, sw,
                      b12s, osbuf, smalls, ready):
    b = p.b
    A = p.banks
    q3 = qs.rearrange("p (c t) -> p c t", c=NTILE)
    k3 = ks.rearrange("p (c t) -> p c t", c=8)
    v3 = vs.rearrange("p (c t) -> p c t", c=NTILE)
    m3 = masks.rearrange("p (c t) -> p c t", c=16)
    e3 = ebuf.rearrange("p (c t) -> p c t", c=4)
    ac3 = attn_c.rearrange("p (c t) -> p c t", c=NTILE)
    sm = smalls
    first = True
    prev_evac_dve = 0
    for h in range(16):
        vh = h // 2
        for slot in (0, 1):
            kts = SLOT_KTS[slot]
            n = len(kts)
            exp_m = {}
            pvm = {}
            maskd = {}
            for i, kt in enumerate(kts):
                par = i % 2
                bS1 = A[4 + par]; bS2 = A[6]
                w = []
                if first:
                    w += ready
                    first = False
                if i >= 2:
                    w.append(("act", exp_m[i - 2]))
                if i == 0 and prev_evac_dve:
                    w.append(("dve", prev_evac_dve))
                kb = 64 * (h % 2)
                t1, b1_ = qpos(2 * h)
                t2, b2_ = qpos(2 * h + 1)
                assert b1_ == kb and b2_ == kb
                b.op("pe", lambda e, h=h, slot=slot, kt=kt, bS1=bS1, kb=kb,
                     t1=t1: e.matmul(
                    bS1.ap()[:, :],
                    k3[:, h // 2, kt * 128:(kt + 1) * 128][kb:kb + 64, :],
                    q3[:, t1, slot * 512:(slot + 1) * 512][kb:kb + 64, :],
                    start=True, stop=True), waits=w)
                w2 = []
                if i >= 1:
                    w2.append(("act", exp_m[i - 1]))
                b.op("pe", lambda e, h=h, slot=slot, kt=kt, bS2=bS2, kb=kb,
                     t2=t2: e.matmul(
                    bS2.ap()[:, :],
                    k3[:, h // 2, kt * 128:(kt + 1) * 128][kb:kb + 64, :],
                    q3[:, t2, slot * 512:(slot + 1) * 512][kb:kb + 64, :],
                    start=True, stop=True), waits=w2, inc="pe")
                pm = b.v("pe")
                wa = [("pe", pm)]
                if i >= 2:
                    wa.append(("pe", pvm[i - 2]))
                b.op("act", lambda e, h=h, par=par, bS1=bS1: e.activation(
                    out=e3[:, par, :], in_=bS1.ap()[:, :], func=AF.Exp,
                    bias=negBs.ap()[:, 2 * h:2 * h + 1], scale=0.125), waits=wa)
                b.op("act", lambda e, h=h, par=par, bS2=bS2: e.activation(
                    out=e3[:, 2 + par, :], in_=bS2.ap()[:, :], func=AF.Exp,
                    bias=negBs.ap()[:, 2 * h + 1:2 * h + 2], scale=0.125),
                    inc="act")
                exp_m[i] = b.v("act")
                mi = MASKED.get((slot, kt))
                if mi is not None:
                    b.op("dve", lambda e, par=par, mi=mi: e.tensor_mul(
                        e3[:, par, :], e3[:, par, :], m3[:, mi, :]),
                        waits=(("act", exp_m[i]),))
                    b.op("dve", lambda e, par=par, mi=mi: e.tensor_mul(
                        e3[:, 2 + par, :], e3[:, 2 + par, :], m3[:, mi, :]),
                        inc="dve")
                    maskd[i] = ("dve", b.v("dve"))
                else:
                    maskd[i] = ("act", exp_m[i])
                b.op("pe", lambda e, par=par, i=i, n=n: e.matmul(
                    A[2].ap()[0:1, :], p.c_ones_col.ap()[:, 0:1], e3[:, par, :],
                    start=(i == 0), stop=(i == n - 1)), waits=(maskd[i],))
                b.op("pe", lambda e, par=par, i=i, n=n: e.matmul(
                    A[3].ap()[0:1, :], p.c_ones_col.ap()[:, 0:1],
                    e3[:, 2 + par, :], start=(i == 0), stop=(i == n - 1)))
                b.op("pe", lambda e, kt=kt, vh=vh, par=par, i=i, n=n: e.matmul(
                    A[0].ap()[:, :], v3[:, kt, vh * 128:(vh + 1) * 128],
                    e3[:, par, :], start=(i == 0), stop=(i == n - 1)))
                b.op("pe", lambda e, kt=kt, vh=vh, par=par, i=i, n=n: e.matmul(
                    A[1].ap()[:, :], v3[:, kt, vh * 128:(vh + 1) * 128],
                    e3[:, 2 + par, :], start=(i == 0), stop=(i == n - 1)),
                    inc="pe")
                pvm[i] = b.v("pe")
            last_pe = pvm[n - 1]
            b.op("dve", lambda e: e.reciprocal(sm[0:1, 0:512], A[2].ap()[0:1, :]),
                 waits=(("pe", last_pe),))
            b.op("dve", lambda e: e.reciprocal(sm[0:1, 512:1024],
                                               A[3].ap()[0:1, :]), inc="dve")
            dm = b.v("dve")
            b.op("dve", lambda e: e.tensor_scalar_min(
                p.ln_small.ap()[0:1, 1024:1536], sm[0:1, 0:512], 1e30),
                 waits=(("dve", dm),))
            b.op("dve", lambda e: e.tensor_scalar_min(
                p.ln_small.ap()[0:1, 1536:2048], sm[0:1, 512:1024], 1e30),
                 inc="dve")
            dm = b.v("dve")
            b.op("pe", lambda e: e.matmul(A[7].ap()[:, :], p.c_ones_row.ap()[0:1, :],
                                          p.ln_small.ap()[0:1, 1024:1536],
                                          start=True, stop=True),
                 waits=(("dve", dm),), inc="pe")
            pm = b.v("pe")
            b.op("act", lambda e: e.copy(out=b12s[:, 0:512], in_=A[7].ap()[:, :]),
                 waits=(("pe", pm),), inc="act")
            am = b.v("act")
            b.op("pe", lambda e: e.matmul(A[7].ap()[:, :], p.c_ones_row.ap()[0:1, :],
                                          p.ln_small.ap()[0:1, 1536:2048],
                                          start=True, stop=True),
                 waits=(("act", am),), inc="pe")
            pm = b.v("pe")
            b.op("act", lambda e: e.copy(out=b12s[:, 512:1024],
                                         in_=A[7].ap()[:, :]),
                 waits=(("pe", pm),), inc="act")
            am = b.v("act")
            b.op("dve", lambda e: e.scalar_tensor_tensor(
                osbuf[:, 512:1024], A[1].ap()[:, :], lnb[:, 0:1],
                b12s[:, 512:1024], ALU.mult, ALU.mult),
                waits=(("act", am),))
            b.op("dve", lambda e: e.scalar_tensor_tensor(
                osbuf[:, 0:512], A[0].ap()[:, :], 1.0,
                b12s[:, 0:512], ALU.mult, ALU.mult))
            b.op("dve", lambda e: e.tensor_add(
                osbuf[:, 0:512], osbuf[:, 0:512],
                osbuf[:, 512:1024]), inc="dve")
            prev_evac_dve = b.v("dve")
            b.op("act", lambda e: e.activation(
                out=b12s[:, 0:512], in_=osbuf[:, 0:512],
                func=AF.Square), waits=(("dve", prev_evac_dve),), inc="act")
            am = b.v("act")
            b.op("pe", lambda e: e.matmul(
                A[7].ap()[0:1, :], p.c_ones_col.ap()[:, 0:1], b12s[:, 0:512],
                start=True, stop=True), waits=(("act", am),), inc="pe")
            pm = b.v("pe")
            b.op("dve", lambda e: e.tensor_scalar(
                sm[0:1, 0:512], A[7].ap()[0:1, :], 1.0 / 128.0, 1e-5,
                ALU.mult, ALU.add), waits=(("pe", pm),), inc="dve")
            dm = b.v("dve")
            b.op("act", lambda e: e.activation(out=sm[0:1, 0:512],
                                               in_=sm[0:1, 0:512], func=AF.Sqrt),
                 waits=(("dve", dm),), inc="act")
            am = b.v("act")
            b.op("dve", lambda e: e.reciprocal(sm[0:1, 512:1024], sm[0:1, 0:512]),
                 waits=(("act", am),), inc="dve")
            dm = b.v("dve")
            b.op("dve", lambda e: e.tensor_copy(p.ln_small.ap()[0:1, 1024:1536],
                                                sm[0:1, 512:1024]),
                 waits=(("dve", dm),), inc="dve")
            dm = b.v("dve")
            b.op("pe", lambda e: e.matmul(A[7].ap()[:, :], p.c_ones_row.ap()[0:1, :],
                                          p.ln_small.ap()[0:1, 1024:1536],
                                          start=True, stop=True),
                 waits=(("dve", dm),), inc="pe")
            pm = b.v("pe")
            b.op("dve", lambda e, h=h, slot=slot: e.scalar_tensor_tensor(
                ac3[:, h, slot * 512:(slot + 1) * 512],
                osbuf[:, 0:512], sw[:, 0:1], A[7].ap()[:, :],
                ALU.mult, ALU.mult), waits=(("pe", pm),), inc="dve")
            prev_evac_dve = b.v("dve")
    return b.v("dve")


def emit_outproj_residual(p, w_dram, src_slab, x_dram, x1_arena, spill_dram,
                          wbuf, xold, src_ready, dout=D, lowrank=None,
                          lr=None):
    b = p.b
    ndt = dout // 128
    s3 = src_slab.rearrange("p (c t) -> p c t", c=NTILE)
    x13 = x1_arena
    wb3 = wbuf.ap()[:, 0:2 * NTILE * 128].rearrange("p (c n) -> p c n",
                                                    c=2 * NTILE)
    xo3 = xold.rearrange("p (c t) -> p c t", c=2)
    A = p.banks
    evac = {}
    entry_act = b.v("act")
    entry_pe = b.v("pe")
    entry_dve = b.v("dve")
    wsem = "o_wdma"
    xsem = "o_xdma"
    for dt in range(ndt):
        buf = dt % 2
        w = [("pe", entry_pe)] if dt < 2 else [("pe", evac[dt - 2][0])]
        b.op("pool", lambda e, dt=dt, buf=buf: e.dma_start(
            out=wb3[:, buf * NTILE:(buf + 1) * NTILE, :],
            in_=chunked_cols(w_dram, dt * 128, 128)),
            waits=w, inc=wsem, n=16)
        wv = b.v(wsem)
        wx = [("dve", evac[dt - 2][1])] if dt >= 2 else []
        b.op("sync", lambda e, dt=dt, buf=buf: e.dma_start(
            out=xo3[:, buf, :], in_=x_dram.rearrange(
                "(c p) t -> c p t", p=128)[dt, :, :]),
            waits=wx, inc=xsem, n=16)
        xv = b.v(xsem)
        pe_w = [(wsem, wv)]
        if dt == 0 and src_ready:
            pe_w.append(src_ready)
        if dt < 2:
            pe_w.append(("act", entry_act))
            pe_w.append(("dve", entry_dve))
        if dt >= 2:
            pe_w.append(("dve", evac[dt - 2][1]))
        for tch in range(2):
            bank = A[2 * buf + tch]
            for kc in range(NTILE):
                b.op("pe", lambda e, buf=buf, tch=tch, kc=kc, bank=bank: e.matmul(
                    bank.ap()[:, :], wb3[:, buf * NTILE + kc, :],
                    s3[:, kc, tch * 512:(tch + 1) * 512],
                    start=(kc == 0),
                    stop=(lr is None and kc == NTILE - 1)),
                    waits=(pe_w + ([("act", lr[2])] if (lr and dt == 0) else []))
                    if (tch == 0 and kc == 0) else (),
                    inc=None if lr is not None and (tch == 1 and kc == NTILE - 1)
                    else ("pe" if (lr is None and tch == 1 and kc == NTILE - 1)
                          else None))
            if lr is not None:
                b.op("pe", lambda e, dt=dt, tch=tch, bank=bank: e.matmul(
                    bank.ap()[:, :], lr[0][0:16, dt * 128:(dt + 1) * 128],
                    lr[1][0:16, tch * 512:(tch + 1) * 512],
                    start=False, stop=True),
                    inc="pe" if tch == 1 else None)
        if lowrank is not None:
            lowrank(dt)
        pm = b.v("pe")
        for tch in range(2):
            bank = A[2 * buf + tch]
            b.op("dve", lambda e, dt=dt, tch=tch, buf=buf, bank=bank: e.tensor_add(
                x13[:, dt, tch * 512:(tch + 1) * 512], bank.ap()[:, :],
                xo3[:, buf, tch * 512:(tch + 1) * 512]),
                waits=((("pe", pm), (xsem, xv)) if tch == 0 else ()),
                inc="dve" if tch == 1 else None)
        dm = b.v("dve")
        evac[dt] = (pm, dm)
        b.op("sync", lambda e, dt=dt: e.dma_start(
            out=spill_dram.rearrange("(c p) t -> c p t", p=128)[dt, :, :],
            in_=x13[:, dt, :]), waits=(("dve", dm),), inc="spill", n=16)
    return b.v("dve"), b.v("spill")


def emit_ffn0(p, w1T, w3T, w2T, xn, hidden, wbuf, w2buf, x1_dram, x2_dram,
              xold, x2ring, scratch_f32, xn_ready, spill_cnt,
              wscale_up=None, wscale_gate=None, sudp=None, out_scale=SQ8192,
              wscale_dp=None, silu_buf=None, dbg=None):
    b = p.b
    A = p.banks
    xn3 = xn.rearrange("p (c t) -> p c t", c=NTILE)
    h3 = hidden.rearrange("p (c t) -> p c t", c=64)
    wb3 = wbuf.ap()[:, 0:4 * NTILE * 128].rearrange("p (c n) -> p c n",
                                                     c=4 * NTILE)
    w2b3 = w2buf.rearrange("p (c n) -> p c n", c=2 * 32)
    si3 = silu_buf.rearrange("p (c t) -> p c t", c=2)
    sc = scratch_f32
    evac = {}
    entry_pe = b.v("pe")
    entry_act = b.v("act")
    entry_dve = b.v("dve")
    for hs in range(64):
        buf = hs % 2
        w = [("pe", entry_pe)] if hs < 2 else [("pe", evac[hs - 2][0])]
        b.op("pool", lambda e, hs=hs, buf=buf: e.dma_start(
            out=wb3[:, buf * NTILE:(buf + 1) * NTILE, :],
            in_=chunked_cols(w1T, hs * 128, 128)),
            waits=w, inc="f_wdma", n=16)
        b.op("pool", lambda e, hs=hs, buf=buf: e.dma_start(
            out=wb3[:, (2 + buf) * NTILE:(3 + buf) * NTILE, :],
            in_=chunked_cols(w3T, hs * 128, 128)),
            inc="f_wdma", n=16)
        wv = b.v("f_wdma")
        pe_w = [("f_wdma", wv)]
        if hs == 0 and xn_ready:
            pe_w.append(xn_ready)
        if hs < 2:
            pe_w.append(("act", entry_act))
            pe_w.append(("dve", entry_dve))
        if hs >= 2:
            pe_w.append(("dve", evac[hs - 2][1]))
        bb = 4 * buf
        for tch in range(2):
            for kc in range(NTILE):
                b.op("pe", lambda e, buf=buf, tch=tch, kc=kc, bb=bb: e.matmul(
                    A[bb + tch].ap()[:, :], wb3[:, buf * NTILE + kc, :],
                    xn3[:, kc, tch * 512:(tch + 1) * 512],
                    start=(kc == 0), stop=(kc == NTILE - 1)),
                    waits=pe_w if (tch == 0 and kc == 0) else ())
            for kc in range(NTILE):
                b.op("pe", lambda e, buf=buf, tch=tch, kc=kc, bb=bb: e.matmul(
                    A[bb + 2 + tch].ap()[:, :], wb3[:, (2 + buf) * NTILE + kc, :],
                    xn3[:, kc, tch * 512:(tch + 1) * 512],
                    start=(kc == 0), stop=(kc == NTILE - 1)),
                    inc="pe" if (tch == 1 and kc == NTILE - 1) else None)
        pm = b.v("pe")
        for tch in range(2):
            if wscale_gate is not None:
                b.op("act", lambda e, tch=tch, buf=buf, hs=hs, bb=bb: e.activation(
                    out=si3[:, tch, :], in_=A[bb + tch].ap()[:, :], func=AF.Silu,
                    scale=wscale_gate.ap()[:, hs:hs + 1]),
                    waits=((("pe", pm),) if tch == 0 else ()),
                    inc="act" if tch == 1 else None)
            else:
                b.op("act", lambda e, tch=tch, buf=buf, bb=bb: e.activation(
                    out=si3[:, tch, :], in_=A[bb + tch].ap()[:, :], func=AF.Silu),
                    waits=((("pe", pm),) if tch == 0 else ()),
                    inc="act" if tch == 1 else None)
        am = b.v("act")
        for tch in range(2):
            wdve = [("act", am)] if tch == 0 else []
            if hs == 0 and tch == 0 and spill_cnt:
                wdve.append(("spill", spill_cnt))
            if wscale_up is not None:
                b.op("dve", lambda e, tch=tch, hs=hs, bb=bb: e.scalar_tensor_tensor(
                    h3[:, hs, tch * 512:(tch + 1) * 512],
                    A[bb + 2 + tch].ap()[:, :], wscale_up.ap()[:, hs:hs + 1],
                    si3[:, tch, :], ALU.mult, ALU.mult),
                    waits=wdve, inc="dve" if tch == 1 else None)
            else:
                b.op("dve", lambda e, tch=tch, hs=hs, bb=bb: e.scalar_tensor_tensor(
                    h3[:, hs, tch * 512:(tch + 1) * 512],
                    A[bb + 2 + tch].ap()[:, :], 1.0,
                    si3[:, tch, :], ALU.mult, ALU.mult),
                    waits=wdve, inc="dve" if tch == 1 else None)
        dm = b.v("dve")
        evac[hs] = (pm, dm)
    dm_all = b.v("dve")
    if dbg is not None:
        b.op("sync", lambda e: e.dma_start(
            out=dbg["hid_pre"].rearrange("(c p) t -> p c t", p=128),
            in_=h3), waits=(("dve", dm_all),), inc="dbgs", n=16)
        dv = b.v("dbgs")
        b.op("dve", lambda e: None, waits=(("dbgs", dv),))
        b.op("dve", lambda e: e.engine_nop(), inc="dve")
        dm_all = b.v("dve")
    if sudp is not None:
        for hc in range(64):
            b.op("dve", lambda e, hc=hc: e.tensor_scalar_mul(
                h3[:, hc, :], h3[:, hc, :], sudp.ap()[:, hc:hc + 1]),
                inc="dve" if hc == 63 else None)
        dm_all = b.v("dve")
    fw_done = emit_fwht(p, h3, sc, 64, dm_all)
    if dbg is not None:
        b.op("sync", lambda e: e.dma_start(
            out=dbg["hid_post"].rearrange("(c p) t -> p c t", p=128),
            in_=h3), waits=(("dve", fw_done),), inc="dbgs", n=16)
        dv = b.v("dbgs")
        b.op("dve", lambda e: None, waits=(("dbgs", dv),))
        b.op("dve", lambda e: e.engine_nop(), inc="dve")
        fw_done = b.v("dve")
    entry_pe = b.v("pe")
    prev_dve = {}
    prev_pe_group = {}
    osem_war = {}
    for dt in range(NTILE):
        for halfh in range(2):
            slot = (2 * dt + halfh) % 2
            w = [("pe", entry_pe), ("dve", fw_done)] if (2 * dt + halfh) < 2 \
                else [("pe", prev_pe_group[(2 * dt + halfh) - 2])]
            b.op("pool", lambda e, dt=dt, halfh=halfh, slot=slot: e.dma_start(
                out=w2b3[:, slot * 32:(slot + 1) * 32, :],
                in_=w2T[halfh * 4096:(halfh + 1) * 4096,
                        dt * 128:(dt + 1) * 128].rearrange(
                            "(c p) n -> p c n", p=128)),
                waits=w, inc="f2_wdma", n=16)
            wv = b.v("f2_wdma")
            bsel = 2 * (dt % 2)
            pe_w = [("f2_wdma", wv)]
            if dt < 2 and halfh == 0:
                pe_w.append(("dve", fw_done))
            if dt >= 2 and halfh == 0:
                pe_w.append(("dve", prev_dve[dt - 2]))
            for tch in range(2):
                for hc in range(32):
                    b.op("pe", lambda e, dt=dt, halfh=halfh, slot=slot, tch=tch,
                         hc=hc, bsel=bsel: e.matmul(
                        A[bsel + tch].ap()[:, :], w2b3[:, slot * 32 + hc, :],
                        h3[:, halfh * 32 + hc, tch * 512:(tch + 1) * 512],
                        start=(halfh == 0 and hc == 0),
                        stop=(halfh == 1 and hc == 31)),
                        waits=pe_w if (tch == 0 and hc == 0) else (),
                        inc="pe" if (tch == 1 and hc == 31) else None)
            prev_pe_group[2 * dt + halfh] = b.v("pe")
        pm = b.v("pe")
        wx = [("dve", prev_dve[dt - 2])] if dt >= 2 else [("dve", fw_done)]
        b.op("sync", lambda e, dt=dt: e.dma_start(
            out=xold.rearrange("p (c t) -> p c t", c=2)[:, dt % 2, :],
            in_=x1_dram.rearrange("(c p) t -> c p t", p=128)[dt, :, :]),
            waits=wx, inc="fx_dma", n=16)
        xv = b.v("fx_dma")
        for tch in range(2):
            bank = A[2 * (dt % 2) + tch]
            if wscale_dp is not None:
                b.op("dve", lambda e, dt=dt, tch=tch, bank=bank: e.scalar_tensor_tensor(
                    x2ring.rearrange("p (c t) -> p c t", c=2)[:, dt % 2, tch * 512:(tch + 1) * 512],
                    bank.ap()[:, :], wscale_dp.ap()[:, dt:dt + 1],
                    xold.rearrange("p (c t) -> p c t", c=2)[:, dt % 2, tch * 512:(tch + 1) * 512],
                    ALU.mult, ALU.add),
                    waits=((("pe", pm), ("fx_dma", xv)) +
                           ((("osem", osem_war[dt - 2]),) if dt >= 2 else ())
                           if tch == 0 else ()),
                    inc="dve" if tch == 1 else None)
            else:
                b.op("dve", lambda e, dt=dt, tch=tch, bank=bank: e.scalar_tensor_tensor(
                    x2ring.rearrange("p (c t) -> p c t", c=2)[:, dt % 2, tch * 512:(tch + 1) * 512],
                    bank.ap()[:, :], out_scale,
                    xold.rearrange("p (c t) -> p c t", c=2)[:, dt % 2, tch * 512:(tch + 1) * 512],
                    ALU.mult, ALU.add),
                    waits=((("pe", pm), ("fx_dma", xv)) +
                           ((("osem", osem_war[dt - 2]),) if dt >= 2 else ())
                           if tch == 0 else ()),
                    inc="dve" if tch == 1 else None)
        prev_dve[dt] = b.v("dve")
        b.op("sync", lambda e, dt=dt: e.dma_start(
            out=x2_dram.rearrange("(c p) t -> c p t", p=128)[dt, :, :],
            in_=x2ring.rearrange("p (c t) -> p c t", c=2)[:, dt % 2, :]),
            waits=(("dve", prev_dve[dt]),), inc="osem", n=16)
        osem_war[dt] = b.v("osem")
    return b.v("osem")


def emit_fwht(p, h3, sc, nch, hidden_ready):
    b = p.b
    A = p.banks
    nst = {16: 4, 64: 6}[nch]
    for tb in range(8):
        prev_act = b.v("act")
        prev_dve = b.v("dve")
        for c4 in range(nch // 4):
            bank = A[6 + c4 % 2]
            w = []
            if c4 == 0:
                w.append(("dve", hidden_ready if tb == 0 else b.v("dve")))
            if c4 >= 2:
                w.append(("act", evacm[c4 - 2]))
            for j in range(4):
                c = 4 * c4 + j
                b.op("pe", lambda e, c=c, j=j, bank=bank, tb=tb: e.matmul(
                    bank.ap()[:, j * 128:(j + 1) * 128], p.c_h128.ap()[:, :],
                    h3[:, c, tb * 128:(tb + 1) * 128],
                    start=True, stop=True),
                    waits=w if j == 0 else (),
                    inc="pe" if j == 3 else None)
            pm = b.v("pe")
            b.op("act", lambda e, c4=c4, bank=bank: e.copy(
                out=sc[:, c4 * 512:(c4 + 1) * 512], in_=bank.ap()[:, :]),
                waits=(("pe", pm),), inc="act")
            if c4 == 0:
                evacm = {}
            evacm[c4] = b.v("act")
        am = b.v("act")
        first = True
        for st in range(nst):
            g = 1 << st
            na = nch // (2 * g)
            scv = sc.rearrange("p (a two b t) -> p a two b t", a=na, two=2, b=g)
            Av = scv[:, :, 0, :, :]
            Bv = scv[:, :, 1, :, :]
            b.op("dve", lambda e, Av=Av, Bv=Bv: e.tensor_add(Av, Av, Bv),
                 waits=((("act", am),) if first else ()))
            first = False
            b.op("dve", lambda e, Av=Av, Bv=Bv: e.scalar_tensor_tensor(
                Bv, Bv, -2.0, Av, ALU.mult, ALU.add), inc="dve")
        nch_ = nch
        b.op("dve", lambda e, tb=tb, nch_=nch_: e.tensor_copy(
            sb3_scatter(h3, tb),
            sc.rearrange("p (c t) -> p c t", c=nch_)), inc="dve")
    return b.v("dve")


def sb3_scatter(h3, tb):
    return h3[:, :, tb * 128:(tb + 1) * 128]


def build_l0_rest(debug=False):
    p = Prog("l0rest")
    b = p.b
    x_in = p.din("x", [D, T], F32)
    q_in = p.din("q", [D, T], BF16)
    k_in = p.din("kfull", [1024, 2048], BF16)
    v_in = p.din("vfull", [2048, 1024], BF16)
    m_in = p.din("masks", [2048, 512], BF16)
    lam_in = p.din("lam", [1, 256], F32)
    subln_in = p.din("sublnw", [128, 1], F32)
    woT = p.din("woT", [D, D], F32)
    w1T = p.din("w1T", [D, 8192], F32)
    w3T = p.din("w3T", [D, 8192], F32)
    w2T = p.din("w2T", [8192, D], F32)
    x2_out = p.dout("x2", [D, T], F32)
    x1_out = p.dout("x1", [D, T], F32)
    ac_out = p.dout("attn_c", [D, T], BF16) if debug else None

    arena = p.sb("arena", [128, 90112], BF16)
    wbuf = p.sb("wbuf", [128, 8192], BF16)
    p.make_common(rope=False)
    p.c_h128 = p.sb("h128", [128, 128], BF16)
    h128_in = p.din("h128_in", [128, 128], BF16)
    negBs = p.sb("negBs", [128, 64], F32)
    lnb_t = p.sb("lnb", [128, 1], F32)
    sw_t = p.sb("sw", [128, 1], F32)
    smax = p.sb("smax", [1, 256], F32)
    lsc_t = p.sb("lsc", [1, 256], F32)

    ar = arena.ap()
    arf = ar.bitcast(F32)
    qs = ar[:, 0:16384]
    ks = ar[:, 16384:32768]
    vs = ar[:, 32768:49152]
    attn_c = ar[:, 49152:65536]
    masks = ar[:, 65536:73728]
    ebuf = ar[:, 73728:75776]
    b12s = ar[:, 75776:76800]
    silu_b = ar[:, 81920:82944]
    osbuf = arf[:, 38400:39424]
    smalls_ap = arf[0:1, 39424:40448]
    x1_arena = arf[:, 0:16384].rearrange("p (c t) -> p c t", c=NTILE)
    xn2 = ar[:, 65536:81920]
    rbuf2 = ar[:, 55296:57344]
    stats2 = arf[0:1, 24576:27648]
    hidden = ar[:, 0:65536]
    fwht_sc = arf[:, 32768:40960]
    x2ring = arf[:, 40960:43008]
    xold = arf[:, 43008:45056]
    w2buf = ar[:, 65536:73728]
    b.op("sync", lambda e: e.dma_start(
        out=sb3_like(qs, NTILE), in_=q_in.rearrange("(c p) t -> p c t", p=128)),
        inc="adma", n=16)
    b.op("sync", lambda e: e.dma_start(
        out=sb3_like(ks, 8), in_=k_in.rearrange("(c p) t -> p c t", p=128)),
        inc="adma", n=16)
    b.op("sync", lambda e: e.dma_start(
        out=sb3_like(vs, NTILE), in_=v_in.rearrange("(c p) t -> p c t", p=128)),
        inc="adma", n=16)
    b.op("sync", lambda e: e.dma_start(
        out=masks.rearrange("p (c n) -> p c n", c=16),
        in_=m_in.rearrange("(c p) t -> p c t", p=128)),
        inc="adma", n=16)
    b.op("sync", lambda e: e.dma_start(out=p.c_h128.ap()[:, :], in_=h128_in),
        inc="adma", n=16)
    ad = b.v("adma")

    lam_m = emit_l0_lambda(p, lam_in, subln_in, lsc_t.ap(), lnb_t.ap(),
                           sw_t.ap())
    bounds_m = emit_l0_bounds(p, qs, ks, attn_c[:, 0:2048], smax, negBs,
                              ready=[("adma", ad)])
    ready = [("adma", ad), ("act", max(lam_m, bounds_m))]
    attn_m = emit_l0_attention(p, qs, ks, vs, attn_c, masks, ebuf, negBs,
                               lnb_t.ap(), sw_t.ap(), b12s, osbuf,
                               smalls_ap, ready)
    if debug:
        b.op("sync", lambda e: e.dma_start(
            out=ac_out.rearrange("(c p) t -> p c t", p=128),
            in_=sb3_like(attn_c, NTILE)),
            waits=(("dve", attn_m),), inc="spill", n=16)
        nb_out = p.dout("negBs_dbg", [128, 64], F32)
        sm_out = p.dout("smax_dbg", [1, 256], F32)
        ls_out = p.dout("lnb_dbg", [128, 1], F32)
        ls_out2 = p.dout("sw_dbg", [128, 1], F32)
        b.op("sync", lambda e: e.dma_start(out=nb_out, in_=negBs.ap()[:, :]),
             waits=(("dve", attn_m),), inc="spill", n=16)
        b.op("sync", lambda e: e.dma_start(out=sm_out, in_=smax.ap()[:, :]),
             waits=(("dve", attn_m),), inc="spill", n=16)
        b.op("sync", lambda e: e.dma_start(out=ls_out, in_=lnb_t.ap()[:, :]),
             waits=(("dve", attn_m),), inc="spill", n=16)
        b.op("sync", lambda e: e.dma_start(out=ls_out2, in_=sw_t.ap()[:, :]),
             waits=(("dve", attn_m),), inc="spill", n=16)

    op_m, spill_cnt = emit_outproj_residual(
        p, woT, attn_c, x_in, x1_arena, x1_out, wbuf, xold, ("dve", attn_m))

    if debug:
        p.dbg_sq = p.dout("sq_dbg", [D, T], BF16)
        p.dbg_xt = p.dout("xt_dbg", [D, T], F32)
    ln2_m = emit_layernorm(p, x1_arena,
                           xn2.rearrange("p (c t) -> p c t", c=NTILE),
                           stats2, rbuf2,
                           entry=(("dve", op_m),))

    spill_all = b.v("spill")
    dbg = None
    if debug:
        dbg = {"hid_pre": p.dout("hid_pre", [8192, T], BF16),
               "hid_post": p.dout("hid_post", [8192, T], BF16)}
        xn2_out = p.dout("xn2_dbg", [D, T], BF16)
        st_out = p.dout("stats_dbg", [1, 3072], F32)
        lns_out = p.dout("lns_dbg", [1, 2048], F32)
        b.op("sync", lambda e: e.dma_start(out=st_out, in_=stats2),
             waits=(("dve", ln2_m),), inc="spill", n=16)
        lns_f = arf[0:1, 28672:30720]
        b.op("dve", lambda e: e.tensor_copy(lns_f, p.ln_small.ap()[:, :]),
             waits=(("dve", ln2_m),), inc="dve")
        _d = b.v("dve")
        b.op("sync", lambda e: e.dma_start(out=lns_out, in_=lns_f),
             waits=(("dve", _d),), inc="spill", n=16)
        b.op("sync", lambda e: e.dma_start(
            out=xn2_out.rearrange("(c p) t -> p c t", p=128),
            in_=xn2.rearrange("p (c t) -> p c t", c=NTILE)),
            waits=(("dve", ln2_m),), inc="spill", n=16)
        spill_all = b.v("spill")
    o_cnt = emit_ffn0(p, w1T, w3T, w2T, xn2, hidden, wbuf, w2buf,
                      x1_out, x2_out, xold, x2ring, fwht_sc,
                      ("dve", ln2_m), spill_all, silu_buf=silu_b, dbg=dbg)

    b.op("sync", lambda e: None,
         waits=(("osem", o_cnt), ("spill", spill_all)))
    return p


def sb3_like(ap2, nch):
    return ap2.rearrange("p (c t) -> p c t", c=nch)


def emit_lowrank_u(p, a_dram, xn, u_sb, xn_ready, sem_prefix, bank):
    b = p.b
    A = p.banks
    xn3 = xn.rearrange("p (c t) -> p c t", c=NTILE)
    wsem = sem_prefix + "_wdma"
    entry_pe = b.v("pe")
    b.op("pool", lambda e: e.dma_start(
        out=p.lrbuf.ap().rearrange("p (c n) -> p c n", c=NTILE),
        in_=a_dram.rearrange("(c p) n -> p c n", p=128)),
        waits=((("pe", entry_pe),) if entry_pe else ()), inc=wsem, n=16)
    wv = b.v(wsem)
    lr3 = p.lrbuf.ap().rearrange("p (c n) -> p c n", c=NTILE)
    entry_act = b.v("act")
    xr = [xn_ready] if xn_ready else []
    for tch in range(2):
        base = 64 * tch
        for kc in range(NTILE):
            b.op("pe", lambda e, tch=tch, kc=kc, base=base: e.matmul(
                bank.ap()[base:base + 16, :],
                lr3[:, kc, :],
                xn3[:, kc, tch * 512:(tch + 1) * 512],
                start=(kc == 0), stop=(kc == NTILE - 1)),
                waits=([(wsem, wv), ("act", entry_act)] + xr
                       if (tch == 0 and kc == 0) else ()),
                inc="pe" if (tch == 1 and kc == NTILE - 1) else None)
    pm = b.v("pe")
    b.op("act", lambda e: e.copy(out=u_sb[:, 0:512], in_=bank.ap()[0:16, :]),
         waits=(("pe", pm),))
    b.op("act", lambda e: e.copy(out=u_sb[:, 512:1024],
                                 in_=bank.ap()[64:80, :]), inc="act")
    return b.v("act")


def emit_l1_qkv(p, xn, wq, wk, wv, qA, qB, kA, kB, vA, vB, qgain_ap,
                qs, ks, vs_tm, vs_fm, vhat, uq, uk, uv, wbuf, vwbuf,
                xn_ready):
    b = p.b
    A = p.banks
    xn3 = xn.rearrange("p (c t) -> p c t", c=NTILE)
    wb3 = wbuf.ap()[:, 0:2 * NTILE * 128].rearrange("p (c n) -> p c n",
                                                    c=2 * NTILE)
    q3 = qs.rearrange("p (c t) -> p c t", c=NTILE)
    k3 = ks.rearrange("p (c t) -> p c t", c=4)
    vt3 = vs_tm.rearrange("p (c t) -> p c t", c=8)
    vf3 = vs_fm.rearrange("p (c t) -> p c t", c=4)
    vh3 = vhat.rearrange("p (c t) -> p c t", c=4)
    qt3 = sb3(p.qtmp, 2, T)
    sm = p.l1small
    um = emit_lowrank_u(p, qA, xn, uq, xn_ready, "uq", A[6])
    um = emit_lowrank_u(p, kA, xn, uk, None, "uk", A[6])
    um = emit_lowrank_u(p, vA, xn, uv, None, "uv", A[6])

    def head_norm_rope(dt, src_bank_pair, out3, gain_slice, evacs):
        buf = dt % 2
        pm = b.v("pe")
        for tch in range(2):
            b.op("act", lambda e, tch=tch, buf=buf: e.copy(
                out=qt3[:, buf, tch * 512:(tch + 1) * 512],
                in_=src_bank_pair[tch].ap()[:, :]),
                waits=([("pe", pm)] + evacs if tch == 0 else ()),
                inc="act" if tch == 1 else None)
        am = b.v("act")
        b.op("act", lambda e, buf=buf: e.activation(
            out=p.l1sq.rearrange("p (c t) -> p c t", c=2)[:, buf, :],
            in_=qt3[:, buf, :], func=AF.Square), inc="act")
        am2 = b.v("act")
        dw = b.v("dve")
        for tch in range(2):
            b.op("pe", lambda e, tch=tch, buf=buf: e.matmul(
                A[4 + tch].ap()[0:1, :], p.c_ones_col.ap()[:, 0:1],
                p.l1sq.rearrange("p (c t) -> p c t", c=2)[:, buf,
                                                          tch * 512:(tch + 1) * 512],
                start=True, stop=True),
                waits=((("act", am2), ("dve", dw)) if tch == 0 else ()),
                inc="pe" if tch == 1 else None)
        pm2 = b.v("pe")
        for tch in range(2):
            b.op("dve", lambda e, tch=tch: e.tensor_scalar(
                sm[0:1, tch * 512:(tch + 1) * 512], A[4 + tch].ap()[0:1, :],
                1.0 / 128.0, 1.1920929e-07, ALU.mult, ALU.add),
                waits=((("pe", pm2),) if tch == 0 else ()),
                inc="dve" if tch == 1 else None)
        dm = b.v("dve")
        b.op("act", lambda e: e.activation(out=sm[0:1, 0:1024],
                                           in_=sm[0:1, 0:1024], func=AF.Sqrt),
             waits=(("dve", dm),), inc="act")
        am3 = b.v("act")
        b.op("dve", lambda e: e.reciprocal(sm[0:1, 0:1024], sm[0:1, 0:1024]),
             waits=(("act", am3),), inc="dve")
        dm = b.v("dve")
        if gain_slice is not None:
            b.op("dve", lambda e: e.tensor_scalar(
                p.ln_small.ap()[0:1, 0:1024], sm[0:1, 0:1024], gain_slice,
                None, ALU.mult), waits=(("dve", dm),), inc="dve")
        else:
            b.op("dve", lambda e: e.tensor_copy(p.ln_small.ap()[0:1, 0:1024],
                                                sm[0:1, 0:1024]),
                 waits=(("dve", dm),), inc="dve")
        dm = b.v("dve")
        for tch in range(2):
            b.op("pe", lambda e, tch=tch: e.matmul(
                A[6 + tch].ap()[:, :], p.c_ones_row.ap()[0:1, :],
                p.ln_small.ap()[0:1, tch * 512:(tch + 1) * 512],
                start=True, stop=True),
                waits=((("dve", dm),) if tch == 0 else ()),
                inc="pe" if tch == 1 else None)
        pm3 = b.v("pe")
        for tch in range(2):
            b.op("dve", lambda e, tch=tch, buf=buf, dt=dt: e.tensor_mul(
                out3[:, dt, tch * 512:(tch + 1) * 512][64:128, :],
                qt3[:, buf, tch * 512:(tch + 1) * 512][64:128, :],
                A[6 + tch].ap()[64:128, :]),
                waits=((("pe", pm3),) if tch == 0 else ()))
            b.op("dve", lambda e, tch=tch, buf=buf: e.tensor_mul(
                p.l1pre.rearrange("p (c t) -> p c t", c=2)[:, buf,
                                                           tch * 512:(tch + 1) * 512][0:64, :],
                qt3[:, buf, tch * 512:(tch + 1) * 512][0:64, :],
                A[6 + tch].ap()[0:64, :]),
                inc="dve" if tch == 1 else None)
        src = p.l1pre.rearrange("p (c t) -> p c t", c=2)[:, buf, :]
        dst = out3[:, dt, :]
        C = p.c_rope1c.ap(); SS = p.c_rope1s.ap()
        tmp = p.ropetmp.ap()
        b.op("dve", lambda e: e.tensor_mul(dst[0:32, :], src[0:32, :], C[0:32, :]))
        b.op("dve", lambda e: e.tensor_mul(tmp[0:32, :], src[32:64, :],
                                           SS[32:64, :]))
        b.op("dve", lambda e: e.tensor_add(dst[0:32, :], dst[0:32, :],
                                           tmp[0:32, :]))
        b.op("dve", lambda e: e.tensor_mul(dst[32:64, :], src[32:64, :],
                                           C[32:64, :]))
        b.op("dve", lambda e: e.tensor_mul(tmp[32:64, :], src[0:32, :],
                                           SS[0:32, :]))
        b.op("dve", lambda e: e.tensor_add(dst[32:64, :], dst[32:64, :],
                                           tmp[32:64, :]), inc="dve")
        return b.v("dve")

    def proj_heads(w_dram, lrB, u_sb, out3, ndt, gain, sem_prefix):
        evac = {}
        entry_pe = b.v("pe")
        for dt in range(ndt):
            buf = dt % 2
            w = [("pe", entry_pe)] if dt < 2 else [("pe", evac[dt - 2][0])]
            b.op("pool", lambda e, dt=dt, buf=buf: e.dma_start(
                out=wb3[:, buf * NTILE:(buf + 1) * NTILE, :],
                in_=chunked_cols(w_dram, dt * 128, 128)),
                waits=w, inc=sem_prefix + "_wdma", n=16)
            wv_ = b.v(sem_prefix + "_wdma")
            pe_w = [(sem_prefix + "_wdma", wv_)]
            if dt >= 2:
                pe_w.append(("act", evac[dt - 2][1]))
            for tch in range(2):
                bank = A[2 * buf + tch]
                for kc in range(NTILE):
                    b.op("pe", lambda e, buf=buf, tch=tch, kc=kc, bank=bank: e.matmul(
                        bank.ap()[:, :], wb3[:, buf * NTILE + kc, :],
                        xn3[:, kc, tch * 512:(tch + 1) * 512],
                        start=(kc == 0), stop=False),
                        waits=pe_w if (tch == 0 and kc == 0) else ())
                b.op("pe", lambda e, dt=dt, tch=tch, bank=bank, lrB=lrB,
                     u_sb=u_sb: e.matmul(
                    bank.ap()[:, :], lrB[0:16, dt * 128:(dt + 1) * 128],
                    u_sb[0:16, tch * 512:(tch + 1) * 512],
                    start=False, stop=True),
                    inc="pe" if tch == 1 else None)
            pm = b.v("pe")
            evacs = [("dve", evac[dt - 2][2])] if dt >= 2 else []
            dm = head_norm_rope(dt, (A[2 * buf], A[2 * buf + 1]), out3,
                                gain(dt) if gain else None, evacs)
            evac[dt] = (pm, b.v("act"), dm)
        return b.v("dve")

    qm = proj_heads(wq, qB, uq, q3, NTILE,
                    lambda dt: qgain_ap[0:1, dt:dt + 1], "q1")
    km = proj_heads(wk, kB, uk, k3, 4, None, "k1")

    entry_pe = b.v("pe")
    entry_act = b.v("act")
    b.op("pool", lambda e: e.dma_start(
        out=vwbuf.rearrange("p (c n) -> p c n", c=NTILE),
        in_=chunked_cols(wv, 0, 512)),
        waits=(("pe", entry_pe),), inc="v1_wdma", n=16)
    wv_ = b.v("v1_wdma")
    evacm = {}
    for tt in range(8):
        bank = A[4 + tt % 2]
        pe_w = []
        if tt == 0:
            pe_w.append(("v1_wdma", wv_))
            pe_w.append(("act", entry_act))
        if tt >= 2:
            pe_w.append(("act", evacm[tt - 2]))
        for kc in range(NTILE):
            b.op("pe", lambda e, tt=tt, kc=kc, bank=bank: e.matmul(
                bank.ap()[:, :], xn3[:, kc, tt * 128:(tt + 1) * 128],
                vwbuf.rearrange("p (c n) -> p c n", c=NTILE)[:, kc, :],
                start=(kc == 0), stop=False),
                waits=pe_w if kc == 0 else ())
        b.op("pe", lambda e, tt=tt, bank=bank: e.matmul(
            bank.ap()[:, :], uv[0:16, tt * 128:(tt + 1) * 128],
            p.vB_sb.ap()[0:16, :], start=False, stop=True), inc="pe")
        pm = b.v("pe")
        b.op("act", lambda e, tt=tt, bank=bank: e.copy(
            out=vt3[:, tt, :], in_=bank.ap()[:, :]),
            waits=(("pe", pm),), inc="act")
        evacm[tt] = b.v("act")
    vm_tm = b.v("act")
    evac2 = {}
    for dt in range(4):
        bank0, bank1 = A[6], A[7]
        pe_w = []
        if dt >= 1:
            pe_w.append(("act", evac2[dt - 1]))
        for tch in range(2):
            bank = (bank0, bank1)[tch]
            for kc in range(NTILE):
                b.op("pe", lambda e, dt=dt, tch=tch, kc=kc, bank=bank: e.matmul(
                    bank.ap()[:, :],
                    vwbuf.rearrange("p (c n) -> p c n", c=NTILE)[:, kc,
                                                     dt * 128:(dt + 1) * 128],
                    xn3[:, kc, tch * 512:(tch + 1) * 512],
                    start=(kc == 0), stop=False),
                    waits=pe_w if (tch == 0 and kc == 0) else ())
            b.op("pe", lambda e, dt=dt, tch=tch, bank=bank: e.matmul(
                bank.ap()[:, :], p.vB_sb.ap()[0:16, dt * 128:(dt + 1) * 128],
                uv[0:16, tch * 512:(tch + 1) * 512],
                start=False, stop=True), inc="pe" if tch == 1 else None)
        pm = b.v("pe")
        for tch in range(2):
            bank = (bank0, bank1)[tch]
            b.op("act", lambda e, dt=dt, tch=tch, bank=bank: e.copy(
                out=vf3[:, dt, tch * 512:(tch + 1) * 512], in_=bank.ap()[:, :]),
                waits=((("pe", pm),) if tch == 0 else ()),
                inc="act" if tch == 1 else None)
        evac2[dt] = b.v("act")
        am = evac2[dt]
        b.op("act", lambda e, dt=dt: e.activation(
            out=p.l1sq.rearrange("p (c t) -> p c t", c=2)[:, 0, :],
            in_=vf3[:, dt, :], func=AF.Square), inc="act")
        am2 = b.v("act")
        dw = b.v("dve")
        for tch in range(2):
            b.op("pe", lambda e, tch=tch: e.matmul(
                A[4 + tch].ap()[0:1, :], p.c_ones_col.ap()[:, 0:1],
                p.l1sq.rearrange("p (c t) -> p c t", c=2)[:, 0,
                                                          tch * 512:(tch + 1) * 512],
                start=True, stop=True),
                waits=((("act", am2), ("dve", dw)) if tch == 0 else ()),
                inc="pe" if tch == 1 else None)
        pm2 = b.v("pe")
        b.op("act", lambda e: e.activation(out=sm[0:1, 0:512],
                                           in_=A[4].ap()[0:1, :], func=AF.Sqrt),
             waits=(("pe", pm2),))
        b.op("act", lambda e: e.activation(out=sm[0:1, 512:1024],
                                           in_=A[5].ap()[0:1, :], func=AF.Sqrt),
             inc="act")
        am3 = b.v("act")
        b.op("dve", lambda e: e.tensor_scalar_max(sm[0:1, 0:1024],
                                                  sm[0:1, 0:1024], 1e-12),
             waits=(("act", am3),), inc="dve")
        dm = b.v("dve")
        b.op("dve", lambda e: e.reciprocal(sm[0:1, 0:1024], sm[0:1, 0:1024]),
             waits=(("dve", dm),), inc="dve")
        dm = b.v("dve")
        b.op("dve", lambda e: e.tensor_copy(p.ln_small.ap()[0:1, 0:1024],
                                            sm[0:1, 0:1024]),
             waits=(("dve", dm),), inc="dve")
        dm = b.v("dve")
        for tch in range(2):
            b.op("pe", lambda e, tch=tch: e.matmul(
                A[4 + tch].ap()[:, :], p.c_ones_row.ap()[0:1, :],
                p.ln_small.ap()[0:1, tch * 512:(tch + 1) * 512],
                start=True, stop=True),
                waits=((("dve", dm),) if tch == 0 else ()),
                inc="pe" if tch == 1 else None)
        pm3 = b.v("pe")
        for tch in range(2):
            b.op("dve", lambda e, dt=dt, tch=tch: e.tensor_mul(
                vh3[:, dt, tch * 512:(tch + 1) * 512],
                vf3[:, dt, tch * 512:(tch + 1) * 512],
                A[4 + tch].ap()[:, :]),
                waits=((("pe", pm3),) if tch == 0 else ()),
                inc="dve" if tch == 1 else None)
    vhm = b.v("dve")
    return qm, km, vm_tm, vhm


def build_l1_qkv(debug=False):
    p = Prog("l1qkv")
    b = p.b
    x_in = p.din("x", [D, T], F32)
    wq = p.din("wq", [D, D], F32)
    wk = p.din("wk", [D, 512], F32)
    wv = p.din("wv", [D, 512], F32)
    qA = p.din("qA", [D, 16], F32)
    qB = p.din("qB", [16, D], F32)
    kA = p.din("kA", [D, 16], F32)
    kB = p.din("kB", [16, 512], F32)
    vA = p.din("vA", [D, 16], F32)
    vB = p.din("vB", [16, 512], F32)
    rc = p.din("rope1c", [64, T], BF16)
    rs = p.din("rope1s", [64, T], BF16)
    qg = p.din("qgain", [1, 16], F32)
    q_out = p.dout("q1", [D, T], BF16)
    k_out = p.dout("k1", [512, T], BF16)
    v_out = p.dout("v1", [T, 512], BF16)
    vh_out = p.dout("vhat", [512, T], BF16)
    xn_out = p.dout("xn3_dbg", [D, T], BF16) if debug else None

    xt = p.sb("xt", [128, NTILE * T], F32)
    xn = p.sb("xn", [128, NTILE * T], BF16)
    qs = p.sb("qs", [128, NTILE * T], BF16)
    ks = p.sb("ks", [128, 4 * T], BF16)
    vstm = p.sb("vstm", [128, 8 * 512], BF16)
    vsfm = p.sb("vsfm", [128, 4 * T], BF16)
    vhat = p.sb("vhat_sb", [128, 4 * T], BF16)
    wbuf = p.sb("wbuf", [128, 4096], BF16)
    rbuf = p.sb("rbuf", [128, 2 * T], BF16)
    ubuf = p.sb("ubuf", [16, 3 * T], BF16)
    p.lrbuf = p.sb("lrbuf", [128, 256], BF16)
    p.vB_sb = p.sb("vBsb", [16, 512], BF16)
    qB_sb = p.sb("qBsb", [16, D], BF16)
    kB_sb = p.sb("kBsb", [16, 512], BF16)
    
    p.l1pre = p.sb("l1pre", [128, 2 * T], BF16).ap()
    l1small = p.sb("l1small", [1, 1024], F32)
    p.l1small = l1small.ap()
    p.l1sq = rbuf.ap()
    p.make_common()
    p.c_rope1c = p.sb("rope1c_t", [64, T], BF16)
    p.c_rope1s = p.sb("rope1s_t", [64, T], BF16)
    qg_sb = p.sb("qg_sb", [1, 16], F32)
    stats = qs.ap().bitcast(F32)[0:1, 0:3072]

    b.op("sync", lambda e: e.dma_start(
        out=sb3(xt, NTILE, T), in_=x_in.rearrange("(c p) t -> p c t", p=128)),
        inc="adma", n=16)
    b.op("sync", lambda e: e.dma_start(out=p.c_rope1c.ap()[:, :], in_=rc),
         inc="adma", n=16)
    b.op("sync", lambda e: e.dma_start(out=p.c_rope1s.ap()[:, :], in_=rs),
         inc="adma", n=16)
    b.op("sync", lambda e: e.dma_start(out=qg_sb.ap()[:, :], in_=qg),
         inc="adma", n=16)
    b.op("pool", lambda e: e.dma_start(out=p.vB_sb.ap()[:, :], in_=vB),
         inc="adma", n=16)
    b.op("pool", lambda e: e.dma_start(out=qB_sb.ap()[:, :], in_=qB),
         inc="adma", n=16)
    b.op("pool", lambda e: e.dma_start(out=kB_sb.ap()[:, :], in_=kB),
         inc="adma", n=16)
    ad = b.v("adma")
    xn_ready = emit_layernorm(p, sb3(xt, NTILE, T), sb3(xn, NTILE, T),
                              stats, rbuf.ap(), entry=(("adma", ad),))
    u3 = ubuf.ap().rearrange("p (c t) -> p c t", c=3)
    vwbuf = xt.ap().bitcast(BF16)[:, 0:8192]
    qm, km, vm, vhm = emit_l1_qkv(
        p, xn.ap(), wq, wk, wv, qA, qB_sb.ap(), kA, kB_sb.ap(), vA,
        p.vB_sb.ap(), qg_sb.ap(), qs.ap(), ks.ap(), vstm.ap(), vsfm.ap(),
        vhat.ap(), u3[:, 0, :], u3[:, 1, :], u3[:, 2, :], wbuf, vwbuf,
        ("dve", xn_ready))
    b.op("sync", lambda e: e.dma_start(
        out=q_out.rearrange("(c p) t -> p c t", p=128), in_=sb3(qs, NTILE, T)),
        waits=(("dve", qm),), inc="osem", n=16)
    b.op("sync", lambda e: e.dma_start(
        out=k_out.rearrange("(c p) t -> p c t", p=128), in_=sb3(ks, 4, T)),
        waits=(("dve", km),), inc="osem", n=16)
    b.op("sync", lambda e: e.dma_start(
        out=v_out.rearrange("(c p) t -> p c t", p=128), in_=sb3(vstm, 8, 512)),
        waits=(("act", vm),), inc="osem", n=16)
    b.op("sync", lambda e: e.dma_start(
        out=vh_out.rearrange("(c p) t -> p c t", p=128), in_=sb3(vhat, 4, T)),
        waits=(("dve", vhm),), inc="osem", n=16)
    n = 4
    if debug:
        b.op("sync", lambda e: e.dma_start(
            out=xn_out.rearrange("(c p) t -> p c t", p=128),
            in_=sb3(xn, NTILE, T)),
            waits=(("dve", xn_ready),), inc="osem", n=16)
        u_out = p.dout("u_dbg", [16, 3 * T], BF16)
        b.op("sync", lambda e: e.dma_start(out=u_out, in_=ubuf.ap()[:, :]),
             waits=(("dve", qm),), inc="osem", n=16)
        n = 6
    b.op("sync", lambda e: None, waits=(("osem", n * 16),))
    return p


L1SCALE = 128.0 ** -0.5


def emit_l1_negb(p, qg_dram, negB1, scr_row):
    b = p.b
    A = p.banks
    b.op("sync", lambda e: e.dma_start(out=scr_row[0:1, 0:16], in_=qg_dram),
         inc="adma", n=16)
    ad = b.v("adma")
    b.op("act", lambda e: e.activation(out=scr_row[0:1, 16:32],
                                       in_=scr_row[0:1, 0:16], func=AF.Abs),
         waits=(("adma", ad),), inc="act")
    am = b.v("act")
    b.op("dve", lambda e: e.tensor_scalar(
        scr_row[0:1, 16:32], scr_row[0:1, 16:32], -(128.0 * L1SCALE), 44.0,
        ALU.mult, ALU.add), waits=(("act", am),), inc="dve")
    dm = b.v("dve")
    b.op("dve", lambda e: e.tensor_copy(p.ln_small.ap()[0:1, 32:48],
                                        scr_row[0:1, 16:32]),
         waits=(("dve", dm),), inc="dve")
    dm = b.v("dve")
    b.op("pe", lambda e: e.matmul(A[7].ap()[:, 0:16], p.c_ones_row.ap()[0:1, :],
                                  p.ln_small.ap()[0:1, 32:48],
                                  start=True, stop=True),
         waits=(("dve", dm),), inc="pe")
    pm = b.v("pe")
    b.op("act", lambda e: e.copy(out=negB1.ap()[:, :], in_=A[7].ap()[:, 0:16]),
         waits=(("pe", pm),), inc="act")
    return b.v("act")


def emit_l1_attention(p, qs, ks, vs, vhat, attn_c, masks, ebuf, negB1,
                      b12s, osbuf, smalls, ready):
    b = p.b
    A = p.banks
    q3 = qs.rearrange("p (c t) -> p c t", c=NTILE)
    k3 = ks.rearrange("p (c t) -> p c t", c=4)
    v3 = vs.rearrange("p (c t) -> p c t", c=16)
    vh3 = vhat.rearrange("p (c t) -> p c t", c=4)
    m3 = masks.rearrange("p (c t) -> p c t", c=16)
    e3 = ebuf.rearrange("p (c t) -> p c t", c=2)
    ac3 = attn_c.rearrange("p (c t) -> p c t", c=NTILE)
    sm = smalls
    first = True
    prev_evac_dve = 0
    for h in range(16):
        kh = h // 4
        for slot in (0, 1):
            kts = SLOT_KTS[slot]
            n = len(kts)
            exp_m = {}
            pvm = {}
            maskd = {}
            for i, kt in enumerate(kts):
                par = i % 2
                bS = A[4 + par]
                w = []
                if first:
                    w += ready
                    first = False
                if i >= 2:
                    w.append(("act", exp_m[i - 2]))
                if i == 0 and prev_evac_dve:
                    w.append(("dve", prev_evac_dve))
                b.op("pe", lambda e, h=h, slot=slot, kt=kt, kh=kh, bS=bS: e.matmul(
                    bS.ap()[:, :],
                    k3[:, kh, kt * 128:(kt + 1) * 128],
                    q3[:, h, slot * 512:(slot + 1) * 512],
                    start=True, stop=True), waits=w, inc="pe")
                pm = b.v("pe")
                wa = [("pe", pm)]
                if i >= 2:
                    wa.append(("pe", pvm[i - 2]))
                b.op("act", lambda e, h=h, par=par, bS=bS: e.activation(
                    out=e3[:, par, :], in_=bS.ap()[:, :], func=AF.Exp,
                    bias=negB1.ap()[:, h:h + 1], scale=L1SCALE),
                    waits=wa, inc="act")
                exp_m[i] = b.v("act")
                mi = MASKED.get((slot, kt))
                if mi is not None:
                    b.op("dve", lambda e, par=par, mi=mi: e.tensor_mul(
                        e3[:, par, :], e3[:, par, :], m3[:, mi, :]),
                        waits=(("act", exp_m[i]),), inc="dve")
                    maskd[i] = ("dve", b.v("dve"))
                else:
                    maskd[i] = ("act", exp_m[i])
                b.op("pe", lambda e, par=par, i=i, n=n: e.matmul(
                    A[2].ap()[0:1, :], p.c_ones_col.ap()[:, 0:1], e3[:, par, :],
                    start=(i == 0), stop=(i == n - 1)), waits=(maskd[i],))
                b.op("pe", lambda e, kt=kt, kh=kh, par=par, i=i, n=n: e.matmul(
                    A[0].ap()[:, :], v3[:, kt, kh * 128:(kh + 1) * 128],
                    e3[:, par, :], start=(i == 0), stop=(i == n - 1)),
                    inc="pe")
                pvm[i] = b.v("pe")
            last_pe = pvm[n - 1]
            b.op("dve", lambda e: e.reciprocal(sm[0:1, 0:512], A[2].ap()[0:1, :]),
                 waits=(("pe", last_pe),), inc="dve")
            dm = b.v("dve")
            b.op("dve", lambda e: e.tensor_scalar_min(
                p.ln_small.ap()[0:1, 1024:1536], sm[0:1, 0:512], 1e30),
                 waits=(("dve", dm),), inc="dve")
            dm = b.v("dve")
            b.op("dve", lambda e, h=h, slot=slot, kh=kh: e.tensor_mul(
                b12s[:, 0:512], A[0].ap()[:, :],
                vh3[:, kh, slot * 512:(slot + 1) * 512]), inc="dve")
            dm2 = b.v("dve")
            b.op("pe", lambda e: e.matmul(
                A[3].ap()[0:1, :], p.c_ones_col.ap()[:, 0:1], b12s[:, 0:512],
                start=True, stop=True), waits=(("dve", dm2),), inc="pe")
            pm = b.v("pe")
            b.op("dve", lambda e: e.tensor_copy(p.ln_small.ap()[0:1, 1536:2048],
                                                A[3].ap()[0:1, :]),
                 waits=(("pe", pm),), inc="dve")
            dm3 = b.v("dve")
            b.op("pe", lambda e: e.matmul(A[6].ap()[:, :], p.c_ones_row.ap()[0:1, :],
                                          p.ln_small.ap()[0:1, 1536:2048],
                                          start=True, stop=True),
                 waits=(("dve", dm3),))
            b.op("pe", lambda e: e.matmul(A[7].ap()[:, :], p.c_ones_row.ap()[0:1, :],
                                          p.ln_small.ap()[0:1, 1024:1536],
                                          start=True, stop=True), inc="pe")
            pm = b.v("pe")
            b.op("dve", lambda e, h=h, slot=slot, kh=kh: e.tensor_mul(
                osbuf[:, 0:512], vh3[:, kh, slot * 512:(slot + 1) * 512],
                A[6].ap()[:, :]), waits=(("pe", pm),), inc="dve")
            dm4 = b.v("dve")
            b.op("dve", lambda e: e.tensor_sub(
                osbuf[:, 512:1024], A[0].ap()[:, :], osbuf[:, 0:512]),
                waits=(("dve", dm4),), inc="dve")
            dm5 = b.v("dve")
            b.op("dve", lambda e, h=h, slot=slot: e.scalar_tensor_tensor(
                ac3[:, h, slot * 512:(slot + 1) * 512],
                osbuf[:, 512:1024], 1.0, A[7].ap()[:, :],
                ALU.mult, ALU.mult), waits=(("dve", dm5),), inc="dve")
            prev_evac_dve = b.v("dve")
    return b.v("dve")


def build_l1_rest(debug=False):
    p = Prog("l1rest")
    b = p.b
    x_in = p.din("x", [D, T], F32)
    q_in = p.din("q1", [D, T], BF16)
    k_in = p.din("k1full", [512, S], BF16)
    v_in = p.din("v1full", [S, 512], BF16)
    vh_in = p.din("vhat", [512, T], BF16)
    m_in = p.din("masks", [2048, 512], BF16)
    qg = p.din("qgain", [1, 16], F32)
    wproj = p.din("wproj", [D, D], F32)
    pA = p.din("pA", [D, 16], F32)
    pB = p.din("pB", [16, D], F32)
    wupT = p.din("wupT", [D, 8192], F32)
    wgateT = p.din("wgateT", [D, 8192], F32)
    wdownT = p.din("wdownT", [8192, D], F32)
    su_ug = p.din("su_ug", [128, 16], F32)
    su_dp = p.din("su_dp", [128, 64], F32)
    ws_up = p.din("ws_up", [128, 64], F32)
    ws_gate = p.din("ws_gate", [128, 64], F32)
    ws_dp = p.din("ws_dp", [128, 16], F32)
    h128_in = p.din("h128_in", [128, 128], BF16)
    out_x = p.dout("out", [D, T], F32)
    x3_out = p.dout("x3", [D, T], F32)
    ac_out = p.dout("attn_c1", [D, T], BF16) if debug else None
    xf_out = p.dout("xf_dbg", [D, T], BF16) if debug else None

    arena = p.sb("arena", [128, 90112], BF16)
    wbuf = p.sb("wbuf", [128, 8192], BF16)
    p.make_common()
    p.c_h128 = p.sb("h128", [128, 128], BF16)
    negB1 = p.sb("negB1", [128, 16], F32)
    p.lrbuf = p.sb("lrbuf", [128, 256], BF16)
    sut = p.sb("sut", [128, 16], F32)
    sdpt = p.sb("sdpt", [128, 64], F32)
    wsu_t = p.sb("wsut", [128, 64], F32)
    wsg_t = p.sb("wsgt", [128, 64], F32)
    wdp_t = p.sb("wdpt", [128, 16], F32)
    ubuf = p.sb("ubuf", [16, T], BF16)

    ar = arena.ap()
    arf = ar.bitcast(F32)
    qs = ar[:, 0:16384]
    ks = ar[:, 16384:24576]
    vs = ar[:, 24576:32768]
    vhat = ar[:, 32768:36864]
    masks = ar[:, 36864:45056]
    ebuf = ar[:, 45056:46080]
    b12s = ar[:, 46080:47104]
    osbuf = arf[:, 23552:24576]
    smalls_ap = arf[0:1, 24576:25088]
    attn_c = ar[:, 53248:69632]
    x3_arena = arf[:, 0:16384].rearrange("p (c t) -> p c t", c=NTILE)
    xn4 = ar[:, 69632:86016]
    rbuf4 = ar[:, 50176:52224]
    stats4 = arf[0:1, 17920:20992]
    hidden = ar[:, 0:65536]
    fw2_sc = arf[:, 43008:45056]
    fwht_sc = arf[:, 34816:43008]
    x2ring = arf[:, 38912:40960]
    xold = arf[:, 43008:45056]
    w2buf = ar[:, 69632:77824]
    b.op("sync", lambda e: e.dma_start(
        out=sb3_like(qs, NTILE), in_=q_in.rearrange("(c p) t -> p c t", p=128)),
        inc="adma", n=16)
    b.op("sync", lambda e: e.dma_start(
        out=sb3_like(ks, 4), in_=k_in.rearrange("(c p) t -> p c t", p=128)),
        inc="adma", n=16)
    b.op("sync", lambda e: e.dma_start(
        out=sb3_like(vs, 16), in_=v_in.rearrange("(c p) t -> p c t", p=128)),
        inc="adma", n=16)
    b.op("sync", lambda e: e.dma_start(
        out=sb3_like(vhat, 4), in_=vh_in.rearrange("(c p) t -> p c t", p=128)),
        inc="adma", n=16)
    b.op("sync", lambda e: e.dma_start(
        out=masks.rearrange("p (c n) -> p c n", c=16),
        in_=m_in.rearrange("(c p) t -> p c t", p=128)),
        inc="adma", n=16)
    b.op("sync", lambda e: e.dma_start(out=p.c_h128.ap()[:, :], in_=h128_in),
         inc="adma", n=16)
    b.op("sync", lambda e: e.dma_start(out=sut.ap()[:, :], in_=su_ug),
         inc="adma", n=16)
    b.op("sync", lambda e: e.dma_start(out=sdpt.ap()[:, :], in_=su_dp),
         inc="adma", n=16)
    b.op("sync", lambda e: e.dma_start(out=wsu_t.ap()[:, :], in_=ws_up),
         inc="adma", n=16)
    b.op("sync", lambda e: e.dma_start(out=wsg_t.ap()[:, :], in_=ws_gate),
         inc="adma", n=16)
    b.op("sync", lambda e: e.dma_start(out=wdp_t.ap()[:, :], in_=ws_dp),
         inc="adma", n=16)
    ad = b.v("adma")
    b.op("dve", lambda e: e.tensor_scalar_mul(
        wsu_t.ap()[:, :], wsu_t.ap()[:, :], 1.0 / math.sqrt(2048.0)),
        waits=(("adma", ad),), inc="dve")
    dm = b.v("dve")
    b.op("dve", lambda e: e.tensor_mul(wsu_t.ap()[:, :], wsu_t.ap()[:, :],
                                       sdpt.ap()[:, :]),
         waits=(("dve", dm),), inc="dve")
    dm = b.v("dve")
    b.op("dve", lambda e: e.tensor_scalar_mul(
        wsg_t.ap()[:, :], wsg_t.ap()[:, :], 1.0 / math.sqrt(2048.0)),
        waits=(("dve", dm),), inc="dve")
    dm = b.v("dve")
    b.op("dve", lambda e: e.tensor_scalar_mul(
        wdp_t.ap()[:, :], wdp_t.ap()[:, :], SQ8192),
        waits=(("dve", dm),), inc="dve")
    prep_m = b.v("dve")

    nb_m = emit_l1_negb(p, qg, negB1, smalls_ap)
    ready = [("adma", ad), ("act", nb_m)]
    attn_m = emit_l1_attention(p, qs, ks, vs, vhat, attn_c, masks, ebuf,
                               negB1, b12s, osbuf, smalls_ap, ready)
    if debug:
        b.op("sync", lambda e: e.dma_start(
            out=ac_out.rearrange("(c p) t -> p c t", p=128),
            in_=sb3_like(attn_c, NTILE)),
            waits=(("dve", attn_m),), inc="spill", n=16)
    pB_sb_ap = ar[0:16, 47104:49152]
    b.op("pool", lambda e: e.dma_start(out=pB_sb_ap, in_=pB),
         waits=(("dve", attn_m),), inc="adma", n=16)
    ad2 = b.v("adma")
    b.op("pe", lambda e: None, waits=(("adma", ad2),))
    up_m = emit_lowrank_u(p, pA, attn_c, ubuf.ap()[:, :], ("dve", attn_m),
                          "up", p.banks[6])
    op_m, spill_cnt = emit_outproj_residual(
        p, wproj, attn_c, x_in, x3_arena, x3_out, wbuf, xold, ("dve", attn_m),
        lowrank=None, lr=(pB_sb_ap, ubuf.ap(), up_m))
    ln4_m = emit_layernorm(p, x3_arena,
                           xn4.rearrange("p (c t) -> p c t", c=NTILE),
                           stats4, rbuf4,
                           entry=(("dve", op_m),), chunk_scale=sut.ap())
    xf_m = emit_fwht(p, xn4.rearrange("p (c t) -> p c t", c=NTILE),
                     fw2_sc, 16, ln4_m)
    if debug:
        b.op("sync", lambda e: e.dma_start(
            out=xf_out.rearrange("(c p) t -> p c t", p=128),
            in_=xn4.rearrange("p (c t) -> p c t", c=NTILE)),
            waits=(("dve", xf_m),), inc="spill", n=16)
        b.op("dve", lambda e: e.engine_nop(), waits=(("spill", b.v("spill")),))
        b.op("dve", lambda e: e.engine_nop(), inc="dve")
        xf_m = b.v("dve")
    spill_all = b.v("spill")
    dbg = None
    if debug:
        dbg = {"hid_pre": p.dout("hid_pre", [8192, T], BF16),
               "hid_post": p.dout("hid_post", [8192, T], BF16)}
    o_cnt = emit_ffn0(p, wgateT, wupT, wdownT, xn4, hidden, wbuf, w2buf,
                      x3_out, out_x, xold, x2ring, fwht_sc,
                      ("dve", xf_m), spill_all,
                      wscale_up=wsu_t, wscale_gate=wsg_t, sudp=None,
                      wscale_dp=wdp_t, silu_buf=ar[:, 86016:87040], dbg=dbg)
    if debug:
        ws_out = p.dout("wsu_dbg", [128, 144], F32)
        b.op("sync", lambda e: e.dma_start(out=ws_out[:, 0:64], in_=wsu_t.ap()[:, :]),
             waits=(("dve", prep_m),), inc="spill", n=16)
        b.op("sync", lambda e: e.dma_start(out=ws_out[:, 64:128], in_=wsg_t.ap()[:, :]),
             inc="spill", n=16)
        b.op("sync", lambda e: e.dma_start(out=ws_out[:, 128:144], in_=wdp_t.ap()[:, :]),
             inc="spill", n=16)
    b.op("sync", lambda e: None,
         waits=(("osem", o_cnt), ("spill", b.v("spill"))))
    return p



def _hadamard128():
    H = np.array([[1.0]])
    while H.shape[0] < 128:
        H = np.block([[H, H], [H, -H]])
    return H.astype(BF)


_NC_CACHE = {}


def _get_nc(name, builder):
    if name not in _NC_CACHE:
        p = builder(debug=False)
        _NC_CACHE[name] = p.finish()
    return _NC_CACHE[name]


def kernel(**inputs):
    inp = {k: np.asarray(v) for k, v in inputs.items()}
    f32 = np.float32
    cont = np.ascontiguousarray

    x_full = inp["L0_attn_hidden_states"].astype(f32)
    wqT = cont(inp["L0_attn_Wq"].T.astype(f32))
    wkT = cont(inp["L0_attn_Wk"].T.astype(f32))
    wvT = cont(inp["L0_attn_Wv"].T.astype(f32))
    woT = cont(inp["L0_attn_Wo"].T.astype(f32))
    w1T = cont(inp["L0_ffn_W1"].T.astype(f32))
    w3T = cont(inp["L0_ffn_W3"].T.astype(f32))
    w2T = cont(inp["L0_ffn_W2"].T.astype(f32))
    lam = np.concatenate([inp["L0_attn_lambda_q1"], inp["L0_attn_lambda_k1"],
                          inp["L0_attn_lambda_q2"], inp["L0_attn_lambda_k2"]]
                         )[None, :].astype(f32)
    sublnw = inp["L0_attn_subln_weight"].reshape(128, 1).astype(f32)
    wq1 = cont(inp["L1_attn_Wq"].astype(f32))
    wk1 = cont(inp["L1_attn_Wk"].astype(f32))
    wv1 = cont(inp["L1_attn_Wv"].astype(f32))
    qA = cont(inp["L1_attn_q_A"].astype(f32)); qB = cont(inp["L1_attn_q_B"].astype(f32))
    kA = cont(inp["L1_attn_k_A"].astype(f32)); kB = cont(inp["L1_attn_k_B"].astype(f32))
    vA = cont(inp["L1_attn_v_A"].astype(f32)); vB = cont(inp["L1_attn_v_B"].astype(f32))
    qgain = inp["L1_attn_q_gain"][None, :].astype(f32)
    wproj = cont(inp["L1_attn_Wproj"].astype(f32))
    pA = cont(inp["L1_attn_p_A"].astype(f32)); pB = cont(inp["L1_attn_p_B"].astype(f32))
    wupT = cont(inp["L1_ffn_W_up"].T.astype(f32))
    wgateT = cont(inp["L1_ffn_W_gate"].T.astype(f32))
    wdownT = cont(inp["L1_ffn_W_down"].T.astype(f32))
    su_ug = cont(inp["L1_ffn_SU_ug"].reshape(16, 128).T.astype(f32))
    su_dp = cont(inp["L1_ffn_SU_dp"].reshape(64, 128).T.astype(f32))
    ws_up = cont(inp["L1_ffn_Wscale_ug"][:8192].reshape(64, 128).T.astype(f32))
    ws_gate = cont(inp["L1_ffn_Wscale_ug"][8192:].reshape(64, 128).T.astype(f32))
    ws_dp = cont(inp["L1_ffn_Wscale_dp"].reshape(16, 128).T.astype(f32))
    h128 = _hadamard128()
    perm = q_perm_rows()

    cores = list(range(8))
    pos = {h: own_pos(h) for h in (0, 1)}
    r0 = {h: rope0_tables(h) for h in (0, 1)}
    r1 = {h: rope1_tables(h) for h in (0, 1)}
    mk = {h: np.ascontiguousarray(mask_tables(h).reshape(2048, 512))
          for h in (0, 1)}

    x_fm = []
    for c in cores:
        bb, h = divmod(c, 2)
        x_fm.append(cont(x_full[bb][pos[h]].T))

    ncA = _get_nc("A", build_l0_qkv)
    mapsA = []
    for c in cores:
        bb, h = divmod(c, 2)
        mapsA.append({"x": x_fm[c], "wqT": wqT, "wkT": wkT, "wvT": wvT,
                      "rope0c": r0[h][0], "rope0s": r0[h][1]})
    resA = run_bass_kernel_spmd(ncA, mapsA, core_ids=cores)

    kfull = {}
    vfull = {}
    for bb in range(4):
        kvo_e = resA.results[2 * bb]["kv"]
        kvo_o = resA.results[2 * bb + 1]["kv"]
        kg = np.empty((1024, 2048), dtype=kvo_e.dtype)
        kg[:, pos[0]] = kvo_e[0:1024]
        kg[:, pos[1]] = kvo_o[0:1024]
        vg = np.empty((2048, 1024), dtype=kvo_e.dtype)
        vg[pos[0], :] = kvo_e[1024:2048]
        vg[pos[1], :] = kvo_o[1024:2048]
        kfull[bb] = kg
        vfull[bb] = vg

    ncB = _get_nc("B", build_l0_rest)
    mapsB = []
    for c in cores:
        bb, h = divmod(c, 2)
        q = resA.results[c]["q"]
        qp = np.empty_like(q)
        for srow in range(32):
            qp[srow * 64:(srow + 1) * 64] = \
                q[perm[srow] * 64:(perm[srow] + 1) * 64]
        mapsB.append({"x": x_fm[c], "q": qp, "kfull": kfull[bb],
                      "vfull": vfull[bb], "masks": mk[h], "lam": lam,
                      "sublnw": sublnw, "woT": woT, "w1T": w1T, "w3T": w3T,
                      "w2T": w2T, "h128_in": h128})
    resB = run_bass_kernel_spmd(ncB, mapsB, core_ids=cores)

    ncC = _get_nc("C", build_l1_qkv)
    mapsC = []
    for c in cores:
        bb, h = divmod(c, 2)
        mapsC.append({"x": resB.results[c]["x2"], "wq": wq1, "wk": wk1,
                      "wv": wv1, "qA": qA, "qB": qB, "kA": kA, "kB": kB,
                      "vA": vA, "vB": vB, "rope1c": r1[h][0],
                      "rope1s": r1[h][1], "qgain": qgain})
    resC = run_bass_kernel_spmd(ncC, mapsC, core_ids=cores)

    k1full = {}
    v1full = {}
    for bb in range(4):
        ke = resC.results[2 * bb]["k1"]; ko = resC.results[2 * bb + 1]["k1"]
        kg = np.empty((512, 2048), dtype=ke.dtype)
        kg[:, pos[0]] = ke
        kg[:, pos[1]] = ko
        ve = resC.results[2 * bb]["v1"]; vo = resC.results[2 * bb + 1]["v1"]
        vg = np.empty((2048, 512), dtype=ve.dtype)
        vg[pos[0], :] = ve
        vg[pos[1], :] = vo
        k1full[bb] = kg
        v1full[bb] = vg

    ncD = _get_nc("D", build_l1_rest)
    mapsD = []
    for c in cores:
        bb, h = divmod(c, 2)
        mapsD.append({"x": resB.results[c]["x2"], "q1": resC.results[c]["q1"],
                      "k1full": k1full[bb], "v1full": v1full[bb],
                      "vhat": resC.results[c]["vhat"], "masks": mk[h],
                      "qgain": qgain, "wproj": wproj, "pA": pA, "pB": pB,
                      "wupT": wupT, "wgateT": wgateT, "wdownT": wdownT,
                      "su_ug": su_ug, "su_dp": su_dp, "ws_up": ws_up,
                      "ws_gate": ws_gate, "ws_dp": ws_dp, "h128_in": h128})
    resD = run_bass_kernel_spmd(ncD, mapsD, core_ids=cores)

    out = np.empty((4, 2048, 2048), dtype=np.float32)
    for c in cores:
        bb, h = divmod(c, 2)
        out[bb][pos[h]] = resD.results[c]["out"].T
    return out
